# version 27
# baseline (speedup 1.0000x reference)
import numpy as np
from collections import deque
import concourse.bass as bass
import concourse.mybir as mybir
from concourse.bass_utils import run_bass_kernel_spmd
from concourse.bass_core_rust import IndirectOffsetOnAxis
from concourse import library_config

F32 = mybir.dt.float32
BF16 = mybir.dt.bfloat16
I32 = mybir.dt.int32
U32 = mybir.dt.uint32
AF = mybir.ActivationFunctionType
ALU = mybir.AluOpType
AX = mybir.AxisListType

B, T, D = 8, 2048, 1024
NH, DK = 16, 64
E, H, CAP = 8, 4096, 320
DUMP = E * CAP
XROWS = 2688
TC = T // 128
FC = D // 128
HC = H // 128
EPS = 1e-5


class Sem:
    __slots__ = ("h", "n", "dma")

    def __init__(self, h, dma=False):
        self.h = h
        self.n = 0
        self.dma = dma


class Prog:
    ENGINES = ("sync", "scalar", "tensor", "vector", "gpsimd")

    def __init__(self, nc):
        self.nc = nc
        self.q = {e: [] for e in self.ENGINES}

    def sem(self, name, dma=False):
        return Sem(self.nc.alloc_semaphore(name), dma)

    def op(self, eng, build, waits=(), inc=None, inck=1):
        wcap = [(s.h, (s.n if s.dma else v)) for (s, v) in waits
                if (s.n if s.dma else v) > 0]
        if inc is not None:
            inc.n += inck
            icap = (inc.h, inck)
        else:
            icap = None

        def f(h, wcap=wcap, icap=icap, build=build):
            for (sh, v) in wcap:
                h.wait_ge(sh, v)
            ins = build(h)
            if icap is not None:
                ins.then_inc(icap[0], icap[1])
        self.q[eng].append(f)

    def wait(self, eng, sem, val):
        if sem.dma:
            val = sem.n
        if val > 0:
            self.q[eng].append(lambda h, s=sem.h, v=val: h.wait_ge(s, v))

    def finalize(self):
        blk = self.nc.Block()
        with blk as block:
            for e in self.ENGINES:
                fns = self.q[e]
                getattr(block, e)(lambda h, fns=fns: [fn(h) for fn in fns])


def build_core_program():
    nc = bass.Bass("TRN2", target_bir_lowering=False)
    p = Prog(nc)

    dz = nc.dram_tensor("z", [T, D], F32, kind="ExternalInput").ap()
    dwq = nc.dram_tensor("wq", [D, DK], F32, kind="ExternalInput").ap()
    dwk = nc.dram_tensor("wk", [D, DK], F32, kind="ExternalInput").ap()
    dwv = nc.dram_tensor("wv", [D, DK], F32, kind="ExternalInput").ap()
    dfcqw = nc.dram_tensor("fcq_w", [DK, D], F32, kind="ExternalInput").ap()
    dfckw = nc.dram_tensor("fck_w", [DK, D], F32, kind="ExternalInput").ap()
    dfcvw = nc.dram_tensor("fcv_w", [DK, D], F32, kind="ExternalInput").ap()
    dfcqb = nc.dram_tensor("fcq_b", [D], F32, kind="ExternalInput").ap()
    dfckb = nc.dram_tensor("fck_b", [D], F32, kind="ExternalInput").ap()
    dfcvb = nc.dram_tensor("fcv_b", [D], F32, kind="ExternalInput").ap()
    dfcow = nc.dram_tensor("fco_w", [D, D], F32, kind="ExternalInput").ap()
    dfcob = nc.dram_tensor("fco_b", [D], F32, kind="ExternalInput").ap()
    dwr = nc.dram_tensor("Wr", [D, E], F32, kind="ExternalInput").ap()
    dw1 = nc.dram_tensor("W1", [E, HC, 128, FC * 128], F32, kind="ExternalInput").ap()
    db1 = nc.dram_tensor("b1", [E, H], F32, kind="ExternalInput").ap()
    dw2 = nc.dram_tensor("W2", [E, H, D], F32, kind="ExternalInput").ap()
    db2 = nc.dram_tensor("b2", [E, D], F32, kind="ExternalInput").ap()
    dout = nc.dram_tensor("o", [T, D], F32, kind="ExternalOutput").ap()
    dxb = nc.dram_tensor("xb", [XROWS, D], BF16, kind="Internal").ap()
    drsc = nc.dram_tensor("rscr", [1024], F32, kind="Internal").ap()
    dyb = nc.dram_tensor("yb", [XROWS, D], BF16, kind="Internal").ap()
    dybA = nc.dram_tensor("ybA", [1280, D], BF16, kind="Internal").ap()
    dybB = nc.dram_tensor("ybB", [1280, D], BF16, kind="Internal").ap()


    def A(n, s, d):
        return nc.alloc_sbuf_tensor(n, s, d).ap()

    Xtok = A("Xtok", [128, TC * D], F32)
    XT = A("XT", [128, FC * T], BF16)
    WA2 = A("WA2", [128, 6144], BF16)
    A3 = A("A3", [128, FC * DUMP], BF16)
    qT2 = A("qT2", [128, T], BF16)
    VOh2 = A("VOh2", [128, 2 * TC * 65], BF16)
    Pt = A("Pt", [128, 2 * 1024], BF16)
    rb = A("rb", [128, 1024], F32)
    G0 = A("G0", [128, 2 * 1024], BF16)
    G1 = A("G1", [128, 2 * 1024], BF16)
    xtp = A("xtp", [128, 256], F32)
    ysb = A("ysb", [128, 2 * 1024], BF16)
    sqscr = A("sqscr", [128, 1024], BF16)
    zqT = G0[0:64, 0:T]
    zqTp = G0[0:128, 0:T]
    zkT = G1[0:64, 0:T]
    zkTp = G1[0:128, 0:T]
    zvT = ysb[0:64, 0:T]
    zvTp = ysb[0:128, 0:T]
    aoscr = sqscr[0:64, 0:1024]

    wqb = WA2[:, 0:512]
    wkb = WA2[:, 512:1024]
    wvb = WA2[:, 1024:1536]
    fcqw = WA2[0:64, 1536:2560]
    fcqwp = WA2[0:128, 1536:2560]
    fckw = WA2[0:64, 2560:3584]
    fckwp = WA2[0:128, 2560:3584]
    fcvw = WA2[0:64, 3584:4608]
    fcvwp = WA2[0:128, 3584:4608]
    fcqb = A("fcqb", [128, FC], F32)
    fckb = A("fckb", [128, FC], F32)
    fcvbr = A("fcvbr", [1, D], BF16)
    fcvb128 = A("fcvb128", [128, D], BF16)
    fcobr = A("fcobr", [1, D], BF16)
    wrb = A("wrb", [128, FC * E], F32)
    b1sb = A("b1sb", [128, E * HC], F32)
    b2r = A("b2r", [1, D], BF16)

    identf = A("identf", [128, 128], F32)
    onesf = A("onesf", [128, 128], F32)
    u128 = A("u128", [128, 128], BF16)
    onesb = A("onesb", [128, 128], BF16)
    onesrow = A("onesrow", [1, 128], BF16)
    iotaE = A("iotaE", [128, E], F32)
    dumpc = A("dumpc", [128, 1], F32)
    epsc = A("epsc", [128, 1], F32)

    rr = A("rr", [128, TC * E], F32)
    top8 = A("top8", [128, TC * E], F32)
    idxu = A("idxu", [128, TC * E], U32)
    idxf = A("idxf", [128, TC * E], F32)
    sel1 = rb[:, 584:712]
    sel2 = rb[:, 712:840]
    selb = VOh2[:, 0:128]
    negm = xtp[:, 0:16]
    e8 = rb[:, 840:968]
    ssum = xtp[:, 16:32]
    rcps = xtp[:, 32:48]
    p12 = A("p12", [128, 2 * TC], F32)
    csumf = rb[0:1, 256:384]
    pfA = rb[0:1, 0:128]
    pfB = rb[0:1, 128:256]
    cpflat = A("cpflat", [1, 128], BF16)
    nf = rb[:, 976:984]
    tmp8 = rb[:, 968:976]
    posb = rb[:, 384:416]
    keepb = rb[:, 416:448]
    poscb = rb[:, 448:480]
    coefb = rb[:, 480:512]
    coefbA = rb[:, 416:448]
    coefbB = rb[:, 448:480]
    ehmask = rb[:, 384:400]
    clamps = rb[:, 400:416]
    slotb = rb[:, 512:544]
    sslotb = rb[:, 544:576]
    gidx = A("gidx", [128, 2 * TC], I32)
    gidxA = A("gidxA", [128, 2 * TC], I32)
    gidxB = A("gidxB", [128, 2 * TC], I32)
    sidx = A("sidx", [128, 2 * TC], I32)

    lnms = A("lnms", [128, 112], F32)
    mmean = lnms[:, 0:16]
    meansq = lnms[:, 16:32]
    varv = lnms[:, 32:48]
    sqv = lnms[:, 48:64]
    rstd = lnms[:, 64:80]
    msum = lnms[:, 80:96]
    sumsq = lnms[:, 96:112]

    PS = nc.alloc_psum_tensor("PS", [128, 4096], F32).ap()

    sT = p.sem("sT")
    sV = p.sem("sV")
    sS = p.sem("sS")
    sG = p.sem("sG")
    dZ = p.sem("dZ", dma=True)
    dXZ = p.sem("dXZ", dma=True)
    dBS = p.sem("dBS", dma=True)
    dWL = p.sem("dWL", dma=True)
    dAO = p.sem("dAO", dma=True)
    dBC = p.sem("dBC", dma=True)
    dSCT = p.sem("dSCT", dma=True)
    dXL = [p.sem("dXL0", dma=True), p.sem("dXL1", dma=True)]
    dW1 = [p.sem(f"dW1{i}", dma=True) for i in range(4)]
    dW2 = [p.sem(f"dW2{i}", dma=True) for i in range(4)]
    dB2 = p.sem("dB2", dma=True)
    dGA = [p.sem(f"dGA{i}", dma=True) for i in range(8)]
    dY = [p.sem("dY0", dma=True), p.sem("dY1", dma=True)]
    dOUT = p.sem("dOUT", dma=True)

    Z_CHUNK = []
    for zq4 in range(4):
        p.op("sync", lambda h, zq4=zq4: h.dma_start(
            out=Xtok[:, zq4 * 4 * D:(zq4 + 1) * 4 * D]
                .rearrange("p (c j) -> p c j", c=4),
            in_=dz.rearrange("(c p) j -> p c j", p=128)[:, zq4 * 4:(zq4 + 1) * 4, :]),
            inc=dZ, inck=16)
        Z_CHUNK.append(dZ.n)
    Z_LOAD = dZ.n

    def gdma(dst, src):
        p.op("gpsimd", lambda h, d=dst, s=src: h.dma_start(out=d, in_=s),
             inc=dWL, inck=16)

    gdma(wqb[:, :].rearrange("p (c j) -> p c j", c=FC),
         dwq.rearrange("(c p) j -> p c j", p=128))
    gdma(wkb[:, :].rearrange("p (c j) -> p c j", c=FC),
         dwk.rearrange("(c p) j -> p c j", p=128))
    gdma(wvb[:, :].rearrange("p (c j) -> p c j", c=FC),
         dwv.rearrange("(c p) j -> p c j", p=128))
    QKV_W = dWL.n
    gdma(fcqw[:, :], dfcqw[:, :])
    gdma(fckw[:, :], dfckw[:, :])
    gdma(fcvw[:, :], dfcvw[:, :])
    gdma(fcvbr[:, :], dfcvb[None, :])
    gdma(fcvb128[:, :], dfcvb[None, :].broadcast_to([128, D]))
    gdma(fcobr[:, :], dfcob[None, :])
    gdma(wrb[:, :].rearrange("p (c j) -> p c j", c=FC),
         dwr.rearrange("(c p) j -> p c j", p=128))
    gdma(A3[:, 8192:16384].rearrange("p (c j) -> p c j", c=FC),
         dfcow.rearrange("(c p) j -> p c j", p=128))
    W_LOADS = dWL.n

    p.op("gpsimd", lambda h: h.iota(iotaE[:, :], pattern=[[1, E]], base=0,
                                    channel_multiplier=0,
                                    allow_small_or_imprecise_dtypes=True), inc=sG)
    p.op("gpsimd", lambda h: h.memset(onesb[:, :], 1.0), inc=sG)
    p.op("gpsimd", lambda h: h.memset(onesrow[:, :], 1.0), inc=sG)
    p.op("gpsimd", lambda h: h.memset(dumpc[:, :], float(DUMP)), inc=sG)
    p.op("gpsimd", lambda h: h.memset(epsc[:, :], EPS), inc=sG)
    p.op("gpsimd", lambda h: h.memset(G0[64:128, 0:T], 0.0), inc=sG)
    p.op("gpsimd", lambda h: h.memset(G1[64:128, 0:T], 0.0), inc=sG)
    p.op("gpsimd", lambda h: h.memset(ysb[64:128, 0:T], 0.0), inc=sG)
    p.op("gpsimd", lambda h: h.memset(WA2[64:128, 1536:4608], 0.0), inc=sG)
    p.op("gpsimd", lambda h: h.memset(A3[64:128, 0:T], 0.0), inc=sG)
    p.op("gpsimd", lambda h: h.memset(A3[0:64, T:2 * T], 0.0), inc=sG)
    p.op("gpsimd", lambda h: h.memset(A3[64:128, 2 * T:3 * T], 0.0), inc=sG)
    p.op("gpsimd", lambda h: h.memset(A3[0:64, 3 * T:4 * T], 0.0), inc=sG)
    p.op("gpsimd", lambda h: h.memset(
        VOh2[:, :].rearrange("p (a b) -> p a b", b=65)[:, :, 64:65], 1.0), inc=sG)
    p.op("gpsimd", lambda h: h.memset(
        A3[:, 16384:18464].rearrange("p (a b) -> p a b", b=65)[:, :, 64:65],
        1.0), inc=sG)
    ones_done = sG.n
    p.op("gpsimd", lambda h: h.memset(onesf[:, :], 1.0), inc=sG)
    p.op("gpsimd", lambda h: h.affine_select(
        identf[:, :], onesf[:, :], pattern=[[-1, 128]], compare_op=ALU.is_equal,
        fill=0.0, base=0, channel_multiplier=1), waits=[(sG, ones_done + 1)], inc=sG)
    p.op("gpsimd", lambda h: h.iota(
        xtp[:, 128:256], pattern=[[1, 128]], base=0, channel_multiplier=-1,
        allow_small_or_imprecise_dtypes=True), inc=sG)
    CONST_G = sG.n
    p.op("vector", lambda h: h.tensor_scalar(
        out=u128[:, :], in0=xtp[:, 128:256], scalar1=0.0, scalar2=None,
        op0=ALU.is_gt), waits=[(sG, CONST_G)], inc=sV)
    p.op("vector", lambda h: h.memset(rb[:, :], 0.0),
         waits=[(sG, CONST_G)], inc=sV)
    p.op("vector", lambda h: h.memset(sqscr[:, :], 0.0), inc=sV)
    CONST_V = sV.n
    for a in range(XROWS // 128):
        p.op("sync", lambda h, a=a: h.dma_start(
            out=dxb[a * 128:(a + 1) * 128, :], in_=sqscr[:, 0:D]),
             waits=[(sV, CONST_V)] if a == 0 else [], inc=dXZ, inck=16)
    XB_ZERO = dXZ.n

    bps = PS[:, 0:128]
    bias_jobs = []
    bias_jobs.append((dfcqb.rearrange("(r j) -> r j", j=128), 8, fcqb[:, :]))
    bias_jobs.append((dfckb.rearrange("(r j) -> r j", j=128), 8, fckb[:, :]))
    bias_jobs.append((db1.rearrange("e (r j) -> (e r) j", j=128)[0:128, :], 128,
                      b1sb[:, 0:128]))
    bias_jobs.append((db1.rearrange("e (r j) -> (e r) j", j=128)[128:256, :], 128,
                      b1sb[:, 128:256]))
    bstage_ev = 0
    p.wait("tensor", sV, CONST_V)
    for (srcap, nrows, dstap) in bias_jobs:
        p.op("sync", lambda h, srcap=srcap, nrows=nrows: h.dma_start(
            out=xtp[0:nrows, 0:128], in_=srcap),
            waits=[(sV, bstage_ev)], inc=dBS, inck=16)
        ldv = dBS.n
        p.op("tensor", lambda h, nrows=nrows: h.transpose(
            bps[:, 0:nrows], xtp[0:nrows, 0:128], identf[0:nrows, 0:nrows]),
            waits=[(dBS, ldv), (sV, bstage_ev)], inc=sT)
        mmv = sT.n
        p.op("vector", lambda h, dstap=dstap, nrows=nrows: h.tensor_copy(
            dstap, bps[:, 0:nrows]), waits=[(sT, mmv)], inc=sV)
        bstage_ev = sV.n
    BIAS_V = sV.n

    tps = [PS[:, 2048:2176], PS[:, 2560:2688]]
    evA = [("V", BIAS_V), ("V", BIAS_V)]
    p.wait("tensor", sV, CONST_V)
    XT_FC = [0] * FC
    for i, (fc, tc) in enumerate([(a, b_) for a in range(FC) for b_ in range(TC)]):
        sl = i % 2
        ps = tps[sl]
        evw = ([] if evA[sl] == 0 else
               ([(sV, evA[sl][1])] if evA[sl][0] == "V" else [(sS, evA[sl][1])]))
        p.op("tensor", lambda h, ps=ps, tc=tc, fc=fc: h.transpose(
            ps, Xtok[:, tc * D + fc * 128: tc * D + (fc + 1) * 128], identf[:, :]),
            waits=evw + ([(dZ, Z_CHUNK[tc // 4])] if fc == 0 else []),
            inc=sT)
        mmv = sT.n
        p.op("vector", lambda h, ps=ps, tc=tc, fc=fc: h.tensor_copy(
            XT[:, fc * T + tc * 128: fc * T + (tc + 1) * 128], ps),
            waits=[(sT, mmv)], inc=sV)
        evA[sl] = ("V", sV.n)
        if tc == TC - 1:
            XT_FC[fc] = (sV.n, 0)
    XT_V = sV.n

    zpsum = PS[0:64, 0:2048]
    p.wait("tensor", dWL, W_LOADS)
    p.wait("scalar", dWL, W_LOADS)
    zev = 0
    for (wt, zt) in ((wqb, zqT), (wkb, zkT), (wvb, zvT)):
        p.wait("tensor", sV, zev)
        for fc in range(FC):
            p.wait("tensor", sV, XT_FC[fc][0])
            for ns in range(4):
                p.op("tensor", lambda h, wt=wt, fc=fc, ns=ns: h.matmul(
                    zpsum[:, ns * 512:(ns + 1) * 512],
                    wt[:, fc * DK:(fc + 1) * DK],
                    XT[:, fc * T + ns * 512: fc * T + (ns + 1) * 512],
                    start=(fc == 0), stop=(fc == FC - 1)),
                    inc=(sT if (fc == FC - 1 and ns == 3) else None))
        mmv = sT.n
        p.op("vector", lambda h, zt=zt: h.tensor_copy(zt[:, :], zpsum),
             waits=[(sT, mmv)], inc=sV)
        zev = sV.n
    ZPROJ_V = sV.n
    ZPROJ_T = sT.n

    qkpsum = PS[:, 0:2048]
    vps = [PS[:, 0:128], PS[:, 512:640]]
    SPb = [PS[:, 0:1024], PS[:, 1024:2048]]
    PVp = [PS[0:65, 2048:3072], PS[0:65, 3072:4096]]

    qk_ev = [ZPROJ_V, ZPROJ_V]
    pt_free = [0, 0]
    sp_free = [0, 0]

    vo_evs = [0, 0]
    pend_epi = [None]

    def emit_pv(vost, pts, exp_s, jc, par):
        PV = PVp[par]
        p.op("tensor", lambda h, vost=vost, pts=pts, jc=jc, PV=PV: h.matmul(
            PV[:, 0:512], vost, Pt[:, pts * 1024:pts * 1024 + 512],
            start=(jc == 0), stop=(jc == TC - 1)),
            waits=[(sS, exp_s)] + ([(sV, pv_done_par[par])] if jc == 0 else []))
        p.op("tensor", lambda h, vost=vost, pts=pts, jc=jc, PV=PV: h.matmul(
            PV[:, 512:1024], vost,
            Pt[:, pts * 1024 + 512:pts * 1024 + 1024],
            start=(jc == 0), stop=(jc == TC - 1)), inc=sT)
        pt_free[pts] = sT.n

    def mk_epilogue(pair, hh, ipass, pv_t, par):
        st = {}

        def emitA():
            PV = PVp[par]
            icol = ipass * 1024
            p.op("vector", lambda h, PV=PV: h.reciprocal(rb[64:65, 0:1024],
                                                         PV[64:65, 0:1024]),
                 waits=[(sT, pv_t), (dBC, 0)], inc=sV)
            rcb = sV.n
            dvc = pair
            if hh == 0:
                dst = XT[0:64, dvc * T + icol: dvc * T + icol + 1024]
            else:
                dst = aoscr[:, :]
            p.op("vector", lambda h, dst=dst, PV=PV: h.tensor_copy(
                dst, PV[0:64, 0:1024]),
                waits=[(dAO, ao_dmah[0])], inc=sV)
            pv_done_par[par] = sV.n
            st["dst"] = dst
            st["cp"] = sV.n
            p.op("gpsimd", lambda h: h.dma_start(out=drsc[None, :],
                                                 in_=rb[64:65, 0:1024]),
                 waits=[(sV, rcb)], inc=dBC, inck=16)
            p.op("gpsimd", lambda h: h.dma_start(
                out=rb[0:64, 0:1024],
                in_=drsc[None, :].broadcast_to([64, 1024])),
                 waits=[(sV, pv_done_vh[0]), (dBC, 0)], inc=dBC, inck=16)
            st["bc"] = dBC.n

        def emitB():
            dst = st["dst"]
            p.op("vector", lambda h, dst=dst: h.tensor_mul(
                dst, dst, rb[0:64, 0:1024]),
                waits=[(dBC, st["bc"]), (sV, st["cp"])], inc=sV)
            pv_done_vh[0] = sV.n
            if hh == 1:
                dvc = pair
                icol = ipass * 1024
                xdst = XT[64:128, dvc * T + icol: dvc * T + icol + 1024]
                p.op("gpsimd", lambda h, xdst=xdst: h.dma_start(
                    out=xdst, in_=aoscr[:, :]),
                    waits=[(sV, pv_done_vh[0])], inc=dAO, inck=16)
                ao_dmah[0] = dAO.n

        return emitA, emitB

    pv_done_vh = [0]
    pv_done_par = [0, 0]
    pb_gh = [0]
    ao_dmah = [0]
    pend_epiB = [None]
    p.wait("vector", dXZ, XB_ZERO)

    kbufA = [A3[:, 0:T], A3[:, 2 * T:3 * T]]
    kbufB = [A3[:, T:2 * T], A3[:, 3 * T:4 * T]]
    vbuf = [VOh2, A3[:, 16384:18464]]
    kpadA = kbufA[0]
    kpadB = kbufB[0]
    wdead = {}
    gates = {}
    KPREP_V = {}
    VPREP_V = {}

    def mk_kv_items(np):
        kq = deque()
        vq = deque()
        kmm = [0] * 4
        kev = [0] * 4
        vmm = [0] * TC
        vev = [0] * TC
        BA = kbufA[np % 2]
        BB = kbufB[np % 2]
        vb = vbuf[np % 2]
        for r in range(4):
            def kmm_go(r=r):
                w = ([(sV, gates["k"])] if r < 2 else [(sV, kev[r - 2])])
                p.op("tensor", lambda h, r=r, np=np: h.matmul(
                    PS[:, 3072 + (r % 2) * 512:3072 + (r % 2) * 512 + 512],
                    fckwp[:, np * 128:(np + 1) * 128],
                    zkTp[:, r * 512:(r + 1) * 512], start=True, stop=True),
                    waits=w, inc=sT)
                kmm[r] = sT.n
                if np == NH // 2 - 1:
                    wdead["fck"] = sT.n
            kq.append(kmm_go)

            def kev_go(r=r):
                ps = PS[:, 3072 + (r % 2) * 512:3072 + (r % 2) * 512 + 512]
                p.op("vector", lambda h, r=r, np=np, ps=ps: h.tensor_scalar(
                    out=BA[0:64, r * 512:(r + 1) * 512], in0=ps[0:64, :],
                    scalar1=fckb[0:64, np:np + 1], scalar2=None, op0=ALU.add),
                    waits=[(sT, kmm[r])], inc=sV)
                p.op("vector", lambda h, r=r, np=np, ps=ps: h.tensor_scalar(
                    out=BB[64:128, r * 512:(r + 1) * 512], in0=ps[64:128, :],
                    scalar1=fckb[64:128, np:np + 1], scalar2=None, op0=ALU.add),
                    inc=sV)
                kev[r] = sV.n
                KPREP_V[np] = sV.n
            kq.append(kev_go)
        vbias = fcvb128[:, np * 128:(np + 1) * 128].rearrange(
            "p (hh j) -> p hh j", hh=2)
        for tc in range(TC):
            def vmm_go(tc=tc):
                slot = PS[:, 2048 + (tc % 2) * 512:2048 + (tc % 2) * 512 + 128]
                w = ([(sV, gates["v"])] if tc < 2 else [(sV, vev[tc - 2])])
                p.op("tensor", lambda h, tc=tc, np=np, slot=slot: h.matmul(
                    slot, zvTp[:, tc * 128:(tc + 1) * 128],
                    fcvwp[:, np * 128:(np + 1) * 128], start=True, stop=True),
                    waits=w, inc=sT)
                vmm[tc] = sT.n
                if np == NH // 2 - 1:
                    wdead["fcv"] = sT.n
            vq.append(vmm_go)

            def vev_go(tc=tc):
                slot = PS[:, 2048 + (tc % 2) * 512:2048 + (tc % 2) * 512 + 128]
                p.op("vector", lambda h, tc=tc, slot=slot: h.tensor_add(
                    vb[:, :].rearrange("p (hh c j) -> p hh c j",
                                       hh=2, c=TC)[:, :, tc, 0:64],
                    slot.rearrange("p (hh j) -> p hh j", hh=2),
                    vbias),
                    waits=[(sT, vmm[tc])], inc=sV)
                vev[tc] = sV.n
                VPREP_V[np] = sV.n
            vq.append(vev_go)
        return kq, vq

    for pair in range(NH // 2):
        qsl = qT2[:, :]
        p.wait("tensor", sS, sp_free[0])
        p.wait("tensor", sS, sp_free[1])
        p.wait("tensor", sV, qk_ev[0])
        if pair in VPREP_V:
            p.wait("tensor", sV, VPREP_V[pair])
        for ns in range(4):
            p.op("tensor", lambda h, ns=ns, pair=pair: h.matmul(
                qkpsum[:, ns * 512:(ns + 1) * 512],
                fcqwp[:, pair * 128:(pair + 1) * 128],
                zqTp[:, ns * 512:(ns + 1) * 512], start=True, stop=True),
                inc=(sT if ns == 3 else None))
        mmv = sT.n
        if pair == NH // 2 - 1:
            wdead["fcq"] = mmv
        p.op("vector", lambda h, pair=pair: h.tensor_scalar(
            out=qsl, in0=qkpsum, scalar1=fcqb[:, pair:pair + 1], scalar2=None,
            op0=ALU.add), waits=[(sT, mmv)], inc=sV)
        p.wait("tensor", sV, sV.n)
        if True:
            for ns in range(4):
                p.op("tensor", lambda h, ns=ns, pair=pair: h.matmul(
                    qkpsum[:, ns * 512:(ns + 1) * 512],
                    fckwp[:, pair * 128:(pair + 1) * 128],
                    zkTp[:, ns * 512:(ns + 1) * 512], start=True, stop=True),
                    inc=(sT if ns == 3 else None))
            mmv = sT.n
            if pair == NH // 2 - 1:
                wdead["fck"] = mmv
            p.op("vector", lambda h, pair=pair: h.tensor_scalar(
                out=kpadA[0:64, :], in0=qkpsum[0:64, :],
                scalar1=fckb[0:64, pair:pair + 1], scalar2=None,
                op0=ALU.add), waits=[(sT, mmv)], inc=sV)
            p.op("vector", lambda h, pair=pair: h.tensor_scalar(
                out=kpadB[64:128, :], in0=qkpsum[64:128, :],
                scalar1=fckb[64:128, pair:pair + 1], scalar2=None,
                op0=ALU.add), inc=sV)
            p.wait("tensor", sV, sV.n)
            vbias0 = fcvb128[:, pair * 128:(pair + 1) * 128].rearrange(
                "p (hh j) -> p hh j", hh=2)
            for tc in range(TC):
                sl = tc % 2
                ps = vps[sl]
                p.op("tensor", lambda h, ps=ps, tc=tc, pair=pair: h.matmul(
                    ps, zvTp[:, tc * 128:(tc + 1) * 128],
                    fcvwp[:, pair * 128:(pair + 1) * 128], start=True,
                    stop=True),
                    waits=[(sV, vo_evs[sl])], inc=sT)
                mmv = sT.n
                p.op("vector", lambda h, ps=ps, tc=tc, vbias0=vbias0:
                     h.tensor_add(
                         VOh2[:, :].rearrange("p (hh c j) -> p hh c j",
                                              hh=2, c=TC)[:, :, tc, 0:64],
                         ps.rearrange("p (hh j) -> p hh j", hh=2),
                         vbias0),
                    waits=[(sT, mmv)], inc=sV)
                vo_evs[sl] = sV.n
            if pair == NH // 2 - 1:
                wdead["fcv"] = sT.n
            p.wait("tensor", sV, sV.n)

        kq, vq = deque(), deque()
        kA = kbufA[0]
        kB = kbufB[0]
        vB = vbuf[0]
        for hh in range(2):
            for ipass in range(2):
                par = (pair * 4 + hh * 2 + ipass) % 2
                icol = ipass * 1024
                host = (kq if (hh == 1 and ipass == 0) else
                        (vq if (hh == 1 and ipass == 1) else None))
                if hh == 1 and ipass == 1 and (pair + 1) in KPREP_V:
                    p.wait("tensor", sV, KPREP_V[pair + 1])
                pend_pv = None
                for jc in range(TC):
                    sp = SPb[jc % 2]
                    pts = jc % 2
                    kst = (kA if hh == 0 else kB)[:, jc * 128:(jc + 1) * 128]
                    qmv = qsl[:, icol:icol + 1024]
                    p.op("tensor", lambda h, sp=sp, kst=kst, qmv=qmv: h.matmul(
                        sp[:, 0:512], kst, qmv[:, 0:512], start=True, stop=True),
                        waits=[(sS, sp_free[jc % 2])])
                    p.op("tensor", lambda h, sp=sp, kst=kst, qmv=qmv: h.matmul(
                        sp[:, 512:1024], kst, qmv[:, 512:1024],
                        start=True, stop=True), inc=sT)
                    qk_t = sT.n
                    if (pair == NH // 2 - 1 and hh == 1 and ipass == 1
                            and jc == TC - 1):
                        wdead["qT2"] = qk_t
                    if jc == 1 and pend_epi[0] is not None:
                        pend_epi[0][0]()
                        pend_epiB[0] = pend_epi[0][1]
                        pend_epi[0] = None
                        gates["k"] = pv_done_par[1]
                        gates["v"] = pv_done_par[0]
                    if jc == 5 and pend_epiB[0] is not None:
                        pend_epiB[0]()
                        pend_epiB[0] = None
                    p.op("scalar", lambda h, sp=sp, pts=pts: h.activation(
                        Pt[:, pts * 1024:(pts + 1) * 1024], sp, AF.Exp, scale=0.125),
                        waits=[(sT, qk_t), (sT, pt_free[pts])], inc=sS)
                    sp_free[jc % 2] = sS.n
                    exp_s = sS.n
                    if pend_pv is not None:
                        emit_pv(*pend_pv)
                    vost = vB[:, (hh * TC + jc) * 65:(hh * TC + jc + 1) * 65]
                    pend_pv = (vost, pts, exp_s, jc, par)
                    if host is not None and jc >= 4:
                        for _ in range(3):
                            if host:
                                host.popleft()()
                emit_pv(*pend_pv)
                while host:
                    host.popleft()()
                pv_t = sT.n
                pend_epi[0] = mk_epilogue(pair, hh, ipass, pv_t, par)
        qk_ev[0] = pv_done_vh[0]
    if pend_epi[0] is not None:
        pend_epi[0][0]()
        pend_epi[0][1]()
        pend_epi[0] = None
    ATT_V = pv_done_vh[0]
    p.wait("tensor", dAO, ao_dmah[0])
    p.wait("tensor", sV, ATT_V)
    ln1_sq_s = []

    fps = [PS[:, 0:1024], PS[:, 1024:2048]]
    fco_ev = [0, 0]
    ln1_s = 0
    for tc in range(TC):
        ps = fps[tc % 2]
        p.wait("tensor", sV, fco_ev[tc % 2])
        for kc in range(FC):
            ast = XT[:, kc * T + tc * 128: kc * T + (tc + 1) * 128]
            for ns in range(2):
                p.op("tensor", lambda h, ps=ps, ast=ast, kc=kc, ns=ns: h.matmul(
                    ps[:, ns * 512:(ns + 1) * 512], ast,
                    A3[:, 8192 + kc * 1024 + ns * 512:
                       8192 + kc * 1024 + (ns + 1) * 512],
                    start=(kc == 0), stop=False))
        for ns in range(2):
            p.op("tensor", lambda h, ps=ps, ns=ns: h.matmul(
                ps[:, ns * 512:(ns + 1) * 512], onesrow[:, :],
                fcobr[:, ns * 512:(ns + 1) * 512], start=False, stop=True),
                inc=(sT if ns == 1 else None))
        mmv = sT.n
        p.op("vector", lambda h, ps=ps, tc=tc: h.scalar_tensor_tensor(
            out=Xtok[:, tc * D:(tc + 1) * D], in0=ps, scalar=1.0,
            in1=Xtok[:, tc * D:(tc + 1) * D], op0=ALU.mult, op1=ALU.add,
            accum_out=msum[:, tc:tc + 1]),
            waits=[(sT, mmv)], inc=sV)
        fco_ev[tc % 2] = sV.n
        prev = sV.n
        p.op("scalar", lambda h, tc=tc: h.activation(
            sqscr[:, :], Xtok[:, tc * D:(tc + 1) * D], AF.Square,
            accum_out=sumsq[:, tc:tc + 1]),
            waits=[(sV, prev), (sS, ln1_s)], inc=sS)
        ln1_s = sS.n
        ln1_sq_s.append(sS.n)
    FCO_V = sV.n
    FCO_T = sT.n

    def ln_stats(lo, hi, wait_s, wait_v):
        g = slice(lo, hi)
        p.op("vector", lambda h, g=g: h.tensor_scalar(
            out=mmean[:, g], in0=msum[:, g], scalar1=1.0 / D, scalar2=None,
            op0=ALU.mult), waits=[(sS, wait_s), (sV, wait_v)], inc=sV)
        p.op("vector", lambda h, g=g: h.tensor_mul(meansq[:, g], mmean[:, g],
                                                   mmean[:, g]),
             waits=[(sV, sV.n)], inc=sV)
        p.op("vector", lambda h, g=g: h.tensor_scalar(
            out=varv[:, g], in0=sumsq[:, g], scalar1=1.0 / D, scalar2=None,
            op0=ALU.mult), inc=sV)
        p.op("vector", lambda h, g=g: h.tensor_sub(varv[:, g], varv[:, g],
                                                   meansq[:, g]),
             waits=[(sV, sV.n)], inc=sV)
        vv = sV.n
        p.op("scalar", lambda h, g=g: h.activation(sqv[:, g], varv[:, g],
                                                   AF.Sqrt,
                                                   bias=epsc[:, :], scale=1.0),
             waits=[(sV, vv)], inc=sS)
        p.op("vector", lambda h, g=g: h.reciprocal(rstd[:, g], sqv[:, g]),
             waits=[(sS, sS.n)], inc=sV)
        return sS.n, sV.n

    x2_ev = []
    for grp in range(4):
        _, gv = ln_stats(grp * 4, (grp + 1) * 4, ln1_sq_s[grp * 4 + 3], FCO_V)
        for tc in range(grp * 4, (grp + 1) * 4):
            p.op("vector", lambda h, tc=tc: h.tensor_scalar(
                out=Xtok[:, tc * D:(tc + 1) * D],
                in0=Xtok[:, tc * D:(tc + 1) * D],
                scalar1=mmean[:, tc:tc + 1], scalar2=rstd[:, tc:tc + 1],
                op0=ALU.subtract, op1=ALU.mult),
                waits=[(sV, gv), (sS, ln1_sq_s[tc])], inc=sV)
            x2_ev.append(sV.n)
    X2_V = sV.n

    rps = [PS[:, 0:128], PS[:, 512:640]]
    rrp = PS[:, 1024:1032]
    ps_ev = [0, 0]
    xtp_used = [0, 0]
    rr_ev = 0
    for tc in range(TC):
        for fc in range(FC):
            sl = fc % 2
            ps = rps[sl]
            p.op("tensor", lambda h, ps=ps, tc=tc, fc=fc: h.transpose(
                ps, Xtok[:, tc * D + fc * 128: tc * D + (fc + 1) * 128],
                identf[:, :]),
                waits=[(sV, ps_ev[sl])] + ([(sV, x2_ev[tc])] if fc == 0 else []),
                inc=sT)
            mmv = sT.n
            p.op("vector", lambda h, ps=ps, sl=sl: h.tensor_copy(
                xtp[:, sl * 128:(sl + 1) * 128], ps),
                waits=[(sT, mmv), (sT, xtp_used[sl])], inc=sV)
            ps_ev[sl] = sV.n
            tpv = sV.n
            p.op("tensor", lambda h, fc=fc, sl=sl: h.matmul(
                rrp, xtp[:, sl * 128:(sl + 1) * 128],
                wrb[:, fc * E:(fc + 1) * E], start=(fc == 0), stop=(fc == FC - 1)),
                waits=[(sV, tpv)] + ([(sV, rr_ev)] if fc == 0 else []),
                inc=sT)
            xtp_used[sl] = sT.n
        mmv = sT.n
        p.op("vector", lambda h, tc=tc: h.tensor_copy(rr[:, tc * E:(tc + 1) * E],
                                                      rrp),
             waits=[(sT, mmv)], inc=sV)
        rr_ev = sV.n
    RR_V = sV.n

    W1SLOT = [WA2[:, 0:1024], WA2[:, 1024:2048], WA2[:, 2048:3072],
              qT2[:, 0:1024]]
    W2SLOT = [WA2[:, 5120:6144], WA2[:, 4096:5120], WA2[:, 3072:4096],
              qT2[:, 1024:2048]]
    w1_ld = [ZPROJ_T, wdead["fcq"], wdead["fck"], wdead["qT2"]]
    w2_ld = [0, wdead["fcv"], wdead["fcv"], wdead["qT2"]]
    w1_ldv = {}
    w2_ldv = {}

    def w1_load(e, hc):
        wsl = hc % 4
        p.op("gpsimd", lambda h, e=e, hc=hc, wsl=wsl: h.dma_start(
            out=W1SLOT[wsl], in_=dw1[e, hc, :, :]),
            waits=[(sT, w1_ld[wsl])], inc=dW1[wsl], inck=16)
        w1_ldv[(e, hc)] = dW1[wsl].n

    def w2_load(e, hc):
        wsl = hc % 4
        p.op("gpsimd", lambda h, e=e, hc=hc, wsl=wsl: h.dma_start(
            out=W2SLOT[wsl], in_=dw2[e, hc * 128:(hc + 1) * 128, :]),
            waits=[(sT, w2_ld[wsl])], inc=dW2[wsl], inck=16)
        w2_ldv[(e, hc)] = dW2[wsl].n

    for hc0 in range(4):
        w1_load(0, hc0)
        w2_load(0, hc0)

    p.wait("vector", sG, CONST_G)
    scr1 = A("scr1", [128, 128], F32)
    scr2 = A("scr2", [128, 128], F32)
    cum = A("cum", [128, 128], F32)
    m1s = A("m1s", [128, 16], F32)
    m2s = A("m2s", [128, 16], F32)
    e1c = A("e1c", [128, 16], F32)
    e2c = A("e2c", [128, 16], F32)
    sumE = A("sumE", [128, 16], F32)
    rcpE = A("rcpE", [128, 16], F32)
    p1c = A("p1c", [128, 16], F32)
    p2c = A("p2c", [128, 16], F32)
    pos1 = A("pos1", [128, 16], F32)
    pos2 = A("pos2", [128, 16], F32)
    keep1 = A("keep1", [128, 16], F32)
    keep2 = A("keep2", [128, 16], F32)
    posc1 = A("posc1", [128, 16], F32)
    posc2 = A("posc2", [128, 16], F32)
    slt = A("slt", [128, 16], F32)
    iota128 = A("iota128", [128, 128], F32)
    p.op("gpsimd", lambda h: h.iota(iota128[:, :], pattern=[[0, 16], [1, 8]],
                                    base=0, channel_multiplier=0,
                                    allow_small_or_imprecise_dtypes=True), inc=sG)
    IOTA_G = sG.n
    rr3 = rr[:, :].rearrange("p (t e) -> p t e", e=E)

    def seg(x):
        return x[:, :].rearrange("p (t e) -> p t e", e=E)

    def segb(x16):
        return x16[:, :, None].broadcast_to([128, 16, E])

    def vop(build, waits=()):
        p.op("vector", build, waits=waits, inc=sV)
        p.wait("vector", sV, sV.n)

    def sel_block(msrc, rsrc, eqbuf, selout):
        vop(lambda h: h.reduce_max(msrc[:, :], seg(rsrc), axis=AX.X))
        vop(lambda h: h.tensor_tensor(seg(eqbuf), seg(rsrc), segb(msrc),
                                      ALU.is_equal))
        vop(lambda h: h.tensor_copy(cum[:, :], eqbuf[:, :]))
        for sh in (1, 2, 4):
            vop(lambda h, sh=sh: h.tensor_copy(
                seg(scr2)[:, :, 0:sh], seg(cum)[:, :, 0:sh]))
            vop(lambda h, sh=sh: h.tensor_add(
                seg(scr2)[:, :, sh:E], seg(cum)[:, :, sh:E],
                seg(cum)[:, :, 0:E - sh]))
            vop(lambda h: h.tensor_copy(cum[:, :], scr2[:, :]))
        vop(lambda h: h.tensor_scalar(out=scr2[:, :], in0=cum[:, :], scalar1=1.0,
                                      scalar2=None, op0=ALU.is_equal))
        vop(lambda h: h.tensor_mul(selout[:, :], eqbuf[:, :], scr2[:, :]))

    p.wait("vector", sV, RR_V)
    p.wait("vector", sG, IOTA_G)
    sel_block(m1s, rr, scr1, sel1)
    vop(lambda h: h.tensor_scalar(out=scr1[:, :], in0=sel1[:, :], scalar1=1e30,
                                  scalar2=None, op0=ALU.mult))
    vop(lambda h: h.tensor_sub(scr2[:, :], rr[:, :], scr1[:, :]))
    sel_block(m2s, scr2, scr1, sel2)
    vop(lambda h: h.tensor_mul(scr1[:, :], sel1[:, :], iota128[:, :]))
    vop(lambda h: h.reduce_sum(e1c[:, :], seg(scr1), axis=AX.X))
    vop(lambda h: h.tensor_mul(scr1[:, :], sel2[:, :], iota128[:, :]))
    vop(lambda h: h.reduce_sum(e2c[:, :], seg(scr1), axis=AX.X))
    vop(lambda h: h.tensor_tensor(seg(scr1), rr3, segb(m1s), ALU.subtract))
    p.op("scalar", lambda h: h.activation(e8[:, :], scr1[:, :], AF.Exp,
                                          scale=1.0), waits=[(sV, sV.n)], inc=sS)
    p.wait("vector", sS, sS.n)
    vop(lambda h: h.reduce_sum(sumE[:, :], seg(e8), axis=AX.X))
    vop(lambda h: h.reciprocal(rcpE[:, :], sumE[:, :]))
    vop(lambda h: h.tensor_mul(scr1[:, :], e8[:, :], sel1[:, :]))
    vop(lambda h: h.reduce_sum(p1c[:, :], seg(scr1), axis=AX.X))
    vop(lambda h: h.tensor_mul(p1c[:, :], p1c[:, :], rcpE[:, :]))
    vop(lambda h: h.tensor_mul(scr1[:, :], e8[:, :], sel2[:, :]))
    vop(lambda h: h.reduce_sum(p2c[:, :], seg(scr1), axis=AX.X))
    vop(lambda h: h.tensor_mul(p2c[:, :], p2c[:, :], rcpE[:, :]))
    vop(lambda h: h.tensor_add(selb[:, :], sel1[:, :], sel2[:, :]))
    SELB_V = sV.n

    ccp = PS[0:1, 1536:1664]
    p.op("tensor", lambda h: h.matmul(ccp, onesb[:, 0:1], selb[:, :],
                                      start=True, stop=True),
         waits=[(sV, SELB_V)], inc=sT)
    mmv = sT.n
    p.op("vector", lambda h: h.tensor_copy(csumf[:, :], ccp),
         waits=[(sT, mmv)], inc=sV)
    p.wait("vector", sV, sV.n)
    p.op("vector", lambda h: h.memset(pfA[:, 0:8], 0.0), inc=sV)
    p.op("vector", lambda h: h.tensor_copy(pfA[:, 8:128], csumf[:, 0:120]),
         inc=sV)
    p.wait("vector", sV, sV.n)
    srcb_, dstb_ = pfA, pfB
    for sh in (8, 16, 32, 64):
        p.op("vector", lambda h, s=srcb_, d=dstb_, sh=sh: h.tensor_copy(
            d[:, 0:sh], s[:, 0:sh]), inc=sV)
        p.op("vector", lambda h, s=srcb_, d=dstb_, sh=sh: h.tensor_add(
            d[:, sh:128], s[:, sh:128], s[:, 0:128 - sh]), inc=sV)
        p.wait("vector", sV, sV.n)
        srcb_, dstb_ = dstb_, srcb_
    p.op("vector", lambda h, s=srcb_: h.tensor_copy(cpflat[:, :], s[:, :]),
         inc=sV)
    CP_V = sV.n

    npp = PS[:, 1024:1152]
    p.op("tensor", lambda h: h.matmul(npp, u128[:, :], selb[:, :],
                                      start=True, stop=False),
         waits=[(sV, CP_V)])
    p.op("tensor", lambda h: h.matmul(npp, onesrow[:, :], cpflat[:, :],
                                      start=False, stop=True), inc=sT)
    mmv = sT.n
    p.op("vector", lambda h: h.tensor_copy(scr2[:, :], npp),
         waits=[(sT, mmv)], inc=sV)
    p.wait("vector", sV, sV.n)
    for k, (sel, e_c, p_c, posk, keepk, posck) in enumerate((
            (sel1, e1c, p1c, pos1, keep1, posc1),
            (sel2, e2c, p2c, pos2, keep2, posc2))):
        vop(lambda h, sel=sel: h.tensor_mul(scr1[:, :], sel[:, :], scr2[:, :]))
        vop(lambda h, posk=posk: h.reduce_sum(posk[:, :], seg(scr1), axis=AX.X))
        vop(lambda h, posk=posk, keepk=keepk: h.tensor_scalar(
            out=keepk[:, :], in0=posk[:, :], scalar1=float(CAP), scalar2=None,
            op0=ALU.is_lt))
        vop(lambda h, posk=posk, posck=posck: h.tensor_scalar_min(
            posck[:, :], posk[:, :], float(CAP - 1)))
        vop(lambda h, e_c=e_c, posck=posck: h.scalar_tensor_tensor(
            out=slt[:, :], in0=e_c[:, :], scalar=float(CAP), in1=posck[:, :],
            op0=ALU.mult, op1=ALU.add))
        vop(lambda h, k=k: h.tensor_copy(
            gidx[:, :].rearrange("p (t two) -> p t two", two=2)[:, :, k:k + 1],
            slt[:, :, None]))
        vop(lambda h, p_c=p_c, keepk=keepk, k=k: h.tensor_mul(
            coefb[:, :].rearrange("p (t two) -> p t two", two=2)[:, :, k],
            p_c[:, :], keepk[:, :]))
        vop(lambda h, e_c=e_c: h.tensor_scalar(
            out=ehmask[:, :], in0=e_c[:, :], scalar1=float(E // 2),
            scalar2=None, op0=ALU.is_lt))
        vop(lambda h, k=k: h.tensor_mul(
            coefbA[:, :].rearrange("p (t two) -> p t two", two=2)[:, :, k],
            coefb[:, :].rearrange("p (t two) -> p t two", two=2)[:, :, k],
            ehmask[:, :]))
        vop(lambda h, k=k: h.tensor_sub(
            coefbB[:, :].rearrange("p (t two) -> p t two", two=2)[:, :, k],
            coefb[:, :].rearrange("p (t two) -> p t two", two=2)[:, :, k],
            coefbA[:, :].rearrange("p (t two) -> p t two", two=2)[:, :, k]))
        vop(lambda h: h.tensor_scalar_min(clamps[:, :], slt[:, :],
                                          float(E // 2 * CAP - 1)))
        vop(lambda h, k=k: h.tensor_copy(
            gidxA[:, :].rearrange("p (t two) -> p t two", two=2)[:, :, k:k + 1],
            clamps[:, :, None]))
        vop(lambda h: h.tensor_scalar(
            out=clamps[:, :], in0=slt[:, :], scalar1=float(E // 2 * CAP),
            scalar2=None, op0=ALU.subtract))
        vop(lambda h: h.tensor_scalar_max(clamps[:, :], clamps[:, :], 0.0))
        vop(lambda h, k=k: h.tensor_copy(
            gidxB[:, :].rearrange("p (t two) -> p t two", two=2)[:, :, k:k + 1],
            clamps[:, :, None]))
        vop(lambda h: h.tensor_scalar(out=scr1[:, 0:16], in0=slt[:, :],
                                      scalar1=float(DUMP), scalar2=None,
                                      op0=ALU.subtract))
        vop(lambda h, keepk=keepk: h.tensor_mul(scr1[:, 0:16], scr1[:, 0:16],
                                                keepk[:, :]))
        vop(lambda h: h.tensor_scalar(out=scr1[:, 0:16], in0=scr1[:, 0:16],
                                      scalar1=float(DUMP), scalar2=None,
                                      op0=ALU.add))
        vop(lambda h, k=k: h.tensor_copy(
            sidx[:, :].rearrange("p (t two) -> p t two", two=2)[:, :, k:k + 1],
            scr1[:, 0:16, None]))
    ROUTE_V = sV.n
    for tc in range(TC):
        for k in range(2):
            p.op("gpsimd", lambda h, tc=tc, k=k: h.indirect_dma_start(
                out=dxb[:, :],
                out_offset=IndirectOffsetOnAxis(
                    ap=sidx[:, 2 * tc + k:2 * tc + k + 1], axis=0),
                in_=Xtok[:, tc * D:(tc + 1) * D], in_offset=None,
                bounds_check=DUMP - 1, oob_is_err=False),
                waits=[(sV, ROUTE_V), (dSCT, dSCT.n), (dXZ, XB_ZERO)],
                inc=dSCT, inck=16)
    SCAT_DMA = dSCT.n


    dXB = p.sem("dXB", dma=True)
    SC = DUMP // 128
    p.wait("sync", dSCT, SCAT_DMA)
    p.wait("sync", sT, FCO_T)
    p.wait("sync", sV, FCO_V)
    for fc in range(FC):
        p.op("sync", lambda h, fc=fc: h.dma_start(
            out=A3[:, fc * DUMP:(fc + 1) * DUMP],
            in_=dxb[0:DUMP, fc * 128:(fc + 1) * 128], transpose=True),
            inc=dXB, inck=16)
    XBUF_D = dXB.n

    hps = [PS[:, 0:512], PS[:, 512:1024]]

    yps = [PS[:, 1024:2048], PS[:, 2048:3072], PS[0:64, 3072:4096]]
    h_ev = [0, 0]
    y_ev = [0, 0, 0]
    yst = [0, 0]
    hT_busy_t = 0
    b2_used = [0, 0]
    relu_s = {}

    comb_v = [0, 0]

    def emit_pass1_gathers():
        p.wait("gpsimd", dY[0], Y_HALF[0])
        p.wait("gpsimd", dY[1], Y_HALF[1])
        for tc in range(TC):
            sl = tc % 2
            gd = []
            for k, gt in enumerate((G0, G1)):
                p.op("gpsimd", lambda h, tc=tc, k=k, sl=sl, gt=gt:
                     h.indirect_dma_start(
                         out=gt[:, sl * 1024:(sl + 1) * 1024], out_offset=None,
                         in_=dybA[0:1280, :],
                         in_offset=IndirectOffsetOnAxis(
                             ap=gidxA[:, 2 * tc + k:2 * tc + k + 1], axis=0)),
                     waits=[(sV, comb_v[sl])], inc=dGA[k * 2 + sl], inck=16)
                gd.append(dGA[k * 2 + sl].n)
            p.op("vector", lambda h, tc=tc, sl=sl: h.scalar_tensor_tensor(
                out=Xtok[:, tc * D:(tc + 1) * D],
                in0=G1[:, sl * 1024:(sl + 1) * 1024],
                scalar=coefbA[:, 2 * tc + 1:2 * tc + 2],
                in1=Xtok[:, tc * D:(tc + 1) * D], op0=ALU.mult, op1=ALU.add),
                waits=[(dGA[0 + sl], gd[0]), (dGA[2 + sl], gd[1])],
                inc=sV)
            p.wait("vector", sV, sV.n)
            p.op("vector", lambda h, tc=tc, sl=sl: h.scalar_tensor_tensor(
                out=Xtok[:, tc * D:(tc + 1) * D],
                in0=G0[:, sl * 1024:(sl + 1) * 1024],
                scalar=coefbA[:, 2 * tc:2 * tc + 1],
                in1=Xtok[:, tc * D:(tc + 1) * D], op0=ALU.mult, op1=ALU.add),
                inc=sV)
            comb_v[sl] = sV.n

    for e in range(E):
        if e == E // 2 + 1:
            emit_pass1_gathers()
        p.op("gpsimd", lambda h, e=e: h.dma_start(
            out=b2r[:, :], in_=db2[e, :][None, :]),
            waits=[(sT, b2_used[0])], inc=dB2, inck=16)
        b2_ld = dB2.n
        for step in range(HC + 1):
            if step < HC:
                hc = step
                wsl = hc % 4
                if (e, hc) not in w1_ldv:
                    w1_load(e, hc)
                ldv = w1_ldv[(e, hc)]
                psl = hc % 2
                ps = hps[psl]
                for kc in range(FC):
                    p.op("tensor", lambda h, ps=ps, wsl=wsl, kc=kc, e=e: h.matmul(
                        ps[:, 0:CAP],
                        W1SLOT[wsl][:, kc * 128:(kc + 1) * 128],
                        A3[:, kc * DUMP + e * CAP: kc * DUMP + (e + 1) * CAP],
                        start=(kc == 0), stop=(kc == FC - 1)),
                        waits=([(dW1[wsl], ldv), (sS, h_ev[psl])] if kc == 0 else [])
                              + ([(sT, hT_busy_t), (dXB, XBUF_D)]
                                 if kc == 0 and hc == 0 else []),
                        inc=(sT if kc == FC - 1 else None))
                w1_ld[wsl] = sT.n
                mmv = sT.n
                p.op("scalar", lambda h, ps=ps, e=e, hc=hc: h.activation(
                    XT[:, hc * CAP:(hc + 1) * CAP], ps[:, 0:CAP], AF.Relu,
                    bias=b1sb[:, e * HC + hc:e * HC + hc + 1], scale=1.0),
                    waits=[(sT, mmv)], inc=sS)
                h_ev[psl] = sS.n
                relu_s[(e, hc)] = sS.n
            if step >= 1:
                hc = step - 1
                wsl = hc % 4
                if (e, hc) not in w2_ldv:
                    w2_load(e, hc)
                ldv = w2_ldv[(e, hc)]
                for si, (s0, s1) in enumerate(((0, 128), (128, 256), (256, 320))):
                    for ns in range(2):
                        p.op("tensor", lambda h, hc=hc, wsl=wsl, si=si, s0=s0,
                             s1=s1, ns=ns: h.matmul(
                            yps[si][0:(s1 - s0), ns * 512:(ns + 1) * 512],
                            XT[0:128, hc * CAP + s0: hc * CAP + s1],
                            W2SLOT[wsl][:, ns * 512:(ns + 1) * 512],
                            start=(hc == 0), stop=False),
                            waits=([(dW2[wsl], ldv), (sS, relu_s[(e, hc)])]
                                   if ns == 0 and si == 0 else [])
                                  + ([(sV, y_ev[si])] if hc == 0 and ns == 0 else []),
                            inc=(sT if (si == 2 and ns == 1) else None))
                w2_ld[wsl] = sT.n
        for si, (s0, s1) in enumerate(((0, 128), (128, 256), (256, 320))):
            for ns in range(2):
                p.op("tensor", lambda h, si=si, s0=s0, s1=s1, ns=ns: h.matmul(
                    yps[si][0:(s1 - s0), ns * 512:(ns + 1) * 512],
                    onesrow[0:1, 0:(s1 - s0)],
                    b2r[:, ns * 512:(ns + 1) * 512],
                    start=False, stop=True),
                    waits=[(dB2, b2_ld)] if si == 0 and ns == 0 else [],
                    inc=(sT if ns == 1 else None))
            b2_used[0] = sT.n
            mmv = sT.n
            ysl = si % 2
            p.op("vector", lambda h, si=si, s0=s0, s1=s1, ysl=ysl: h.tensor_copy(
                ysb[0:(s1 - s0), ysl * 1024:(ysl + 1) * 1024], yps[si][0:(s1 - s0), :]),
                waits=[(sT, mmv), (dY[ysl], yst[ysl])], inc=sV)
            y_ev[si] = sV.n
            ydst = (dybA if e < E // 2 else dybB)
            eo = (e if e < E // 2 else e - E // 2)
            p.op("scalar", lambda h, ydst=ydst, eo=eo, s0=s0, s1=s1, ysl=ysl:
                 h.dma_start(
                     out=ydst[eo * CAP + s0: eo * CAP + s1, :],
                     in_=ysb[0:(s1 - s0), ysl * 1024:(ysl + 1) * 1024]),
                waits=[(sV, y_ev[si])], inc=dY[ysl], inck=16)
            yst[ysl] = dY[ysl].n
        if e == E // 2 - 1:
            Y_HALF = (dY[0].n, dY[1].n)
        hT_busy_t = sT.n
    Y_DMA0, Y_DMA1 = dY[0].n, dY[1].n

    ln2_s = 0
    ln2_sq_s = []
    p.wait("gpsimd", dY[0], Y_DMA0)
    p.wait("gpsimd", dY[1], Y_DMA1)
    g_dma = []
    for tc in range(TC):
        sl = tc % 2
        for k, gt in enumerate((G0, G1)):
            p.op("gpsimd", lambda h, tc=tc, k=k, sl=sl, gt=gt: h.indirect_dma_start(
                out=gt[:, sl * 1024:(sl + 1) * 1024], out_offset=None,
                in_=dybB[0:1280, :],
                in_offset=IndirectOffsetOnAxis(
                    ap=gidxB[:, 2 * tc + k:2 * tc + k + 1], axis=0)),
                waits=[(sV, comb_v[sl])], inc=dGA[k * 2 + sl], inck=16)
        g_dma.append((dGA[0 + sl].n, dGA[2 + sl].n))
        p.op("vector", lambda h, tc=tc, sl=sl: h.scalar_tensor_tensor(
            out=Xtok[:, tc * D:(tc + 1) * D], in0=G1[:, sl * 1024:(sl + 1) * 1024],
            scalar=coefbB[:, 2 * tc + 1:2 * tc + 2],
            in1=Xtok[:, tc * D:(tc + 1) * D], op0=ALU.mult, op1=ALU.add),
            waits=[(dGA[0 + sl], g_dma[tc][0]), (dGA[2 + sl], g_dma[tc][1])],
            inc=sV)
        p.wait("vector", sV, sV.n)
        p.op("vector", lambda h, tc=tc, sl=sl: h.scalar_tensor_tensor(
            out=Xtok[:, tc * D:(tc + 1) * D], in0=G0[:, sl * 1024:(sl + 1) * 1024],
            scalar=coefbB[:, 2 * tc:2 * tc + 1],
            in1=Xtok[:, tc * D:(tc + 1) * D], op0=ALU.mult, op1=ALU.add,
            accum_out=msum[:, tc:tc + 1]), inc=sV)
        comb_v[sl] = sV.n
        prev = sV.n
        p.op("scalar", lambda h, tc=tc: h.activation(
            sqscr[:, :], Xtok[:, tc * D:(tc + 1) * D], AF.Square,
            accum_out=sumsq[:, tc:tc + 1]),
            waits=[(sV, prev), (sS, ln2_s)], inc=sS)
        ln2_s = sS.n
        ln2_sq_s.append(sS.n)
    COMB_V = sV.n
    for grp in range(4):
        _, gv = ln_stats(grp * 4, (grp + 1) * 4, ln2_sq_s[grp * 4 + 3], 0)
        for tc in range(grp * 4, (grp + 1) * 4):
            p.op("vector", lambda h, tc=tc: h.tensor_scalar(
                out=Xtok[:, tc * D:(tc + 1) * D],
                in0=Xtok[:, tc * D:(tc + 1) * D],
                scalar1=mmean[:, tc:tc + 1], scalar2=rstd[:, tc:tc + 1],
                op0=ALU.subtract, op1=ALU.mult),
                waits=[(sV, gv), (sS, ln2_sq_s[tc])], inc=sV)
            xv = sV.n
            p.op("sync", lambda h, tc=tc: h.dma_start(
                out=dout[tc * 128:(tc + 1) * 128, :],
                in_=Xtok[:, tc * D:(tc + 1) * D]),
                waits=[(sV, xv)], inc=dOUT, inck=16)
    p.wait("sync", dOUT, dOUT.n)

    p.finalize()
    return nc


_NC_CACHE = None


def kernel(**inputs):
    global _NC_CACHE
    if _NC_CACHE is None:
        _NC_CACHE = build_core_program()
    nc = _NC_CACHE

    z = np.ascontiguousarray(np.asarray(inputs["L0_attn_z"], np.float32))
    names = {
        "wq": "L0_attn_wq", "wk": "L0_attn_wk", "wv": "L0_attn_wv",
        "fcq_w": "L0_attn_fcq_w", "fcq_b": "L0_attn_fcq_b",
        "fck_w": "L0_attn_fck_w", "fck_b": "L0_attn_fck_b",
        "fcv_w": "L0_attn_fcv_w", "fcv_b": "L0_attn_fcv_b",
        "fco_w": "L0_attn_fco_w", "fco_b": "L0_attn_fco_b",
        "Wr": "L0_ffn_Wr", "W1": "L0_ffn_W1", "b1": "L0_ffn_b1",
        "W2": "L0_ffn_W2", "b2": "L0_ffn_b2",
    }
    shared = {k: np.ascontiguousarray(np.asarray(inputs[v], np.float32))
              for k, v in names.items()}
    shared["W1"] = np.ascontiguousarray(
        shared["W1"].reshape(E, FC, 128, HC, 128).transpose(0, 3, 2, 1, 4)
        .reshape(E, HC, 128, FC * 128))
    in_maps = []
    for b in range(B):
        m = dict(shared)
        m["z"] = np.ascontiguousarray(z[b])
        in_maps.append(m)
    res = run_bass_kernel_spmd(nc, in_maps, core_ids=list(range(B)))
    out = np.stack([res.results[b]["o"] for b in range(B)], axis=0)
    return out.astype(np.float32)


# version 28
# speedup vs baseline: 1.1156x; 1.1156x over previous
import numpy as np
from collections import deque
import concourse.bass as bass
import concourse.mybir as mybir
from concourse.bass_utils import run_bass_kernel_spmd
from concourse.bass_core_rust import IndirectOffsetOnAxis
from concourse import library_config

F32 = mybir.dt.float32
BF16 = mybir.dt.bfloat16
I32 = mybir.dt.int32
U32 = mybir.dt.uint32
AF = mybir.ActivationFunctionType
ALU = mybir.AluOpType
AX = mybir.AxisListType

B, T, D = 8, 2048, 1024
NH, DK = 16, 64
E, H, CAP = 8, 4096, 320
DUMP = E * CAP
XROWS = 2688
TC = T // 128
FC = D // 128
HC = H // 128
EPS = 1e-5


class Sem:
    __slots__ = ("h", "n", "dma")

    def __init__(self, h, dma=False):
        self.h = h
        self.n = 0
        self.dma = dma


class Prog:
    ENGINES = ("sync", "scalar", "tensor", "vector", "gpsimd")

    def __init__(self, nc):
        self.nc = nc
        self.q = {e: [] for e in self.ENGINES}

    def sem(self, name, dma=False):
        return Sem(self.nc.alloc_semaphore(name), dma)

    def op(self, eng, build, waits=(), inc=None, inck=1):
        wcap = [(s.h, (s.n if s.dma else v)) for (s, v) in waits
                if (s.n if s.dma else v) > 0]
        if inc is not None:
            inc.n += inck
            icap = (inc.h, inck)
        else:
            icap = None

        def f(h, wcap=wcap, icap=icap, build=build):
            for (sh, v) in wcap:
                h.wait_ge(sh, v)
            ins = build(h)
            if icap is not None:
                ins.then_inc(icap[0], icap[1])
        self.q[eng].append(f)

    def wait(self, eng, sem, val):
        if sem.dma:
            val = sem.n
        if val > 0:
            self.q[eng].append(lambda h, s=sem.h, v=val: h.wait_ge(s, v))

    def finalize(self):
        blk = self.nc.Block()
        with blk as block:
            for e in self.ENGINES:
                fns = self.q[e]
                getattr(block, e)(lambda h, fns=fns: [fn(h) for fn in fns])


def build_core_program():
    nc = bass.Bass("TRN2", target_bir_lowering=False)
    p = Prog(nc)

    dz = nc.dram_tensor("z", [T, D], F32, kind="ExternalInput").ap()
    dwq = nc.dram_tensor("wq", [D, DK], F32, kind="ExternalInput").ap()
    dwk = nc.dram_tensor("wk", [D, DK], F32, kind="ExternalInput").ap()
    dwv = nc.dram_tensor("wv", [D, DK], F32, kind="ExternalInput").ap()
    dfcqw = nc.dram_tensor("fcq_w", [DK, D], F32, kind="ExternalInput").ap()
    dfckw = nc.dram_tensor("fck_w", [DK, D], F32, kind="ExternalInput").ap()
    dfcvw = nc.dram_tensor("fcv_w", [DK, D], F32, kind="ExternalInput").ap()
    dfcqb = nc.dram_tensor("fcq_b", [D], F32, kind="ExternalInput").ap()
    dfckb = nc.dram_tensor("fck_b", [D], F32, kind="ExternalInput").ap()
    dfcvb = nc.dram_tensor("fcv_b", [D], F32, kind="ExternalInput").ap()
    dfcow = nc.dram_tensor("fco_w", [D, D], F32, kind="ExternalInput").ap()
    dfcob = nc.dram_tensor("fco_b", [D], F32, kind="ExternalInput").ap()
    dwr = nc.dram_tensor("Wr", [D, E], F32, kind="ExternalInput").ap()
    dw1 = nc.dram_tensor("W1", [E, HC, 128, FC * 128], F32, kind="ExternalInput").ap()
    db1 = nc.dram_tensor("b1", [E, H], F32, kind="ExternalInput").ap()
    dw2 = nc.dram_tensor("W2", [E, H, D], F32, kind="ExternalInput").ap()
    db2 = nc.dram_tensor("b2", [E, D], F32, kind="ExternalInput").ap()
    dout = nc.dram_tensor("o", [T, D], F32, kind="ExternalOutput").ap()
    dxb = nc.dram_tensor("xb", [XROWS, D], BF16, kind="Internal").ap()
    drsc = nc.dram_tensor("rscr", [1024], F32, kind="Internal").ap()
    dyb = nc.dram_tensor("yb", [XROWS, D], BF16, kind="Internal").ap()


    def A(n, s, d):
        return nc.alloc_sbuf_tensor(n, s, d).ap()

    Xtok = A("Xtok", [128, TC * D], F32)
    XT = A("XT", [128, FC * T], BF16)
    WA2 = A("WA2", [128, 6144], BF16)
    A3 = A("A3", [128, FC * DUMP], BF16)
    qT2 = A("qT2", [128, T], BF16)
    VOh2 = A("VOh2", [128, 2 * TC * 65], BF16)
    Pt = A("Pt", [128, 2 * 1024], BF16)
    rb = A("rb", [128, 1024], F32)
    G0 = A("G0", [128, 2 * 1024], BF16)
    G1 = A("G1", [128, 2 * 1024], BF16)
    xtp = A("xtp", [128, 256], F32)
    ysb = A("ysb", [128, 2 * 1024], BF16)
    sqscr = A("sqscr", [128, 1024], BF16)
    zqT = G0[0:64, 0:T]
    zqTp = G0[0:128, 0:T]
    zkT = G1[0:64, 0:T]
    zkTp = G1[0:128, 0:T]
    zvT = ysb[0:64, 0:T]
    zvTp = ysb[0:128, 0:T]
    aoscr = sqscr[0:64, 0:1024]

    wqb = WA2[:, 0:512]
    wkb = WA2[:, 512:1024]
    wvb = WA2[:, 1024:1536]
    fcqw = WA2[0:64, 1536:2560]
    fcqwp = WA2[0:128, 1536:2560]
    fckw = WA2[0:64, 2560:3584]
    fckwp = WA2[0:128, 2560:3584]
    fcvw = WA2[0:64, 3584:4608]
    fcvwp = WA2[0:128, 3584:4608]
    fcqb = A("fcqb", [128, FC], F32)
    fckb = A("fckb", [128, FC], F32)
    fcvbr = A("fcvbr", [1, D], BF16)
    fcvb128 = A("fcvb128", [128, D], BF16)
    fcobr = A("fcobr", [1, D], BF16)
    wrb = A("wrb", [128, FC * E], F32)
    b1sb = A("b1sb", [128, E * HC], F32)
    b2r = A("b2r", [1, D], BF16)

    identf = A("identf", [128, 128], F32)
    onesf = A("onesf", [128, 128], F32)
    u128 = A("u128", [128, 128], BF16)
    onesb = A("onesb", [128, 128], BF16)
    onesrow = A("onesrow", [1, 128], BF16)
    iotaE = A("iotaE", [128, E], F32)
    dumpc = A("dumpc", [128, 1], F32)
    epsc = A("epsc", [128, 1], F32)

    rr = A("rr", [128, TC * E], F32)
    top8 = A("top8", [128, TC * E], F32)
    idxu = A("idxu", [128, TC * E], U32)
    idxf = A("idxf", [128, TC * E], F32)
    sel1 = rb[:, 584:712]
    sel2 = rb[:, 712:840]
    selb = VOh2[:, 0:128]
    negm = xtp[:, 0:16]
    e8 = rb[:, 840:968]
    ssum = xtp[:, 16:32]
    rcps = xtp[:, 32:48]
    p12 = A("p12", [128, 2 * TC], F32)
    csumf = rb[0:1, 256:384]
    pfA = rb[0:1, 0:128]
    pfB = rb[0:1, 128:256]
    cpflat = A("cpflat", [1, 128], BF16)
    nf = rb[:, 976:984]
    tmp8 = rb[:, 968:976]
    posb = rb[:, 384:416]
    keepb = rb[:, 416:448]
    poscb = rb[:, 448:480]
    coefb = rb[:, 480:512]
    slotb = rb[:, 512:544]
    sslotb = rb[:, 544:576]
    gidx = A("gidx", [128, 2 * TC], I32)
    sidx = A("sidx", [128, 2 * TC], I32)

    lnms = A("lnms", [128, 112], F32)
    mmean = lnms[:, 0:16]
    meansq = lnms[:, 16:32]
    varv = lnms[:, 32:48]
    sqv = lnms[:, 48:64]
    rstd = lnms[:, 64:80]
    msum = lnms[:, 80:96]
    sumsq = lnms[:, 96:112]

    PS = nc.alloc_psum_tensor("PS", [128, 4096], F32).ap()

    sT = p.sem("sT")
    sV = p.sem("sV")
    sS = p.sem("sS")
    sG = p.sem("sG")
    dZ = p.sem("dZ", dma=True)
    dXZ = p.sem("dXZ", dma=True)
    dBS = p.sem("dBS", dma=True)
    dWL = p.sem("dWL", dma=True)
    dAO = p.sem("dAO", dma=True)
    dBC = p.sem("dBC", dma=True)
    dSCT = p.sem("dSCT", dma=True)
    dXL = [p.sem("dXL0", dma=True), p.sem("dXL1", dma=True)]
    dW1 = [p.sem(f"dW1{i}", dma=True) for i in range(4)]
    dW2 = [p.sem(f"dW2{i}", dma=True) for i in range(4)]
    dB2 = p.sem("dB2", dma=True)
    dGA = [p.sem(f"dGA{i}", dma=True) for i in range(8)]
    dY = [p.sem("dY0", dma=True), p.sem("dY1", dma=True)]
    dOUT = p.sem("dOUT", dma=True)

    Z_CHUNK = []
    for zq8 in range(8):
        p.op("sync", lambda h, zq8=zq8: h.dma_start(
            out=Xtok[:, zq8 * 2 * D:(zq8 + 1) * 2 * D]
                .rearrange("p (c j) -> p c j", c=2),
            in_=dz.rearrange("(c p) j -> p c j", p=128)[:, zq8 * 2:(zq8 + 1) * 2, :]),
            inc=dZ, inck=16)
        Z_CHUNK.append(dZ.n)
    Z_LOAD = dZ.n

    def gdma(dst, src):
        p.op("gpsimd", lambda h, d=dst, s=src: h.dma_start(out=d, in_=s),
             inc=dWL, inck=16)

    gdma(wqb[:, :].rearrange("p (c j) -> p c j", c=FC),
         dwq.rearrange("(c p) j -> p c j", p=128))
    gdma(wkb[:, :].rearrange("p (c j) -> p c j", c=FC),
         dwk.rearrange("(c p) j -> p c j", p=128))
    gdma(wvb[:, :].rearrange("p (c j) -> p c j", c=FC),
         dwv.rearrange("(c p) j -> p c j", p=128))
    QKV_W = dWL.n
    gdma(fcqw[:, :], dfcqw[:, :])
    gdma(fckw[:, :], dfckw[:, :])
    gdma(fcvw[:, :], dfcvw[:, :])
    gdma(fcvbr[:, :], dfcvb[None, :])
    gdma(fcvb128[:, :], dfcvb[None, :].broadcast_to([128, D]))
    gdma(fcobr[:, :], dfcob[None, :])
    gdma(wrb[:, :].rearrange("p (c j) -> p c j", c=FC),
         dwr.rearrange("(c p) j -> p c j", p=128))
    gdma(A3[:, 8192:16384].rearrange("p (c j) -> p c j", c=FC),
         dfcow.rearrange("(c p) j -> p c j", p=128))
    W_LOADS = dWL.n

    p.op("gpsimd", lambda h: h.iota(iotaE[:, :], pattern=[[1, E]], base=0,
                                    channel_multiplier=0,
                                    allow_small_or_imprecise_dtypes=True), inc=sG)
    p.op("gpsimd", lambda h: h.memset(onesb[:, :], 1.0), inc=sG)
    p.op("gpsimd", lambda h: h.memset(onesrow[:, :], 1.0), inc=sG)
    p.op("gpsimd", lambda h: h.memset(dumpc[:, :], float(DUMP)), inc=sG)
    p.op("gpsimd", lambda h: h.memset(epsc[:, :], EPS), inc=sG)
    p.op("gpsimd", lambda h: h.memset(G0[64:128, 0:T], 0.0), inc=sG)
    p.op("gpsimd", lambda h: h.memset(G1[64:128, 0:T], 0.0), inc=sG)
    p.op("gpsimd", lambda h: h.memset(ysb[64:128, 0:T], 0.0), inc=sG)
    p.op("gpsimd", lambda h: h.memset(WA2[64:128, 1536:4608], 0.0), inc=sG)
    p.op("gpsimd", lambda h: h.memset(A3[64:128, 0:T], 0.0), inc=sG)
    p.op("gpsimd", lambda h: h.memset(A3[0:64, T:2 * T], 0.0), inc=sG)
    p.op("gpsimd", lambda h: h.memset(A3[64:128, 2 * T:3 * T], 0.0), inc=sG)
    p.op("gpsimd", lambda h: h.memset(A3[0:64, 3 * T:4 * T], 0.0), inc=sG)
    p.op("gpsimd", lambda h: h.memset(
        VOh2[:, :].rearrange("p (a b) -> p a b", b=65)[:, :, 64:65], 1.0), inc=sG)
    p.op("gpsimd", lambda h: h.memset(
        A3[:, 16384:18464].rearrange("p (a b) -> p a b", b=65)[:, :, 64:65],
        1.0), inc=sG)
    ones_done = sG.n
    p.op("gpsimd", lambda h: h.memset(onesf[:, :], 1.0), inc=sG)
    p.op("gpsimd", lambda h: h.affine_select(
        identf[:, :], onesf[:, :], pattern=[[-1, 128]], compare_op=ALU.is_equal,
        fill=0.0, base=0, channel_multiplier=1), waits=[(sG, ones_done + 1)], inc=sG)
    p.op("gpsimd", lambda h: h.iota(
        xtp[:, 128:256], pattern=[[1, 128]], base=0, channel_multiplier=-1,
        allow_small_or_imprecise_dtypes=True), inc=sG)
    CONST_G = sG.n
    p.op("vector", lambda h: h.tensor_scalar(
        out=u128[:, :], in0=xtp[:, 128:256], scalar1=0.0, scalar2=None,
        op0=ALU.is_gt), waits=[(sG, CONST_G)], inc=sV)
    p.op("vector", lambda h: h.memset(rb[:, :], 0.0),
         waits=[(sG, CONST_G)], inc=sV)
    p.op("vector", lambda h: h.memset(sqscr[:, :], 0.0), inc=sV)
    CONST_V = sV.n
    for a in range(XROWS // 128):
        p.op("sync", lambda h, a=a: h.dma_start(
            out=dxb[a * 128:(a + 1) * 128, :], in_=sqscr[:, 0:D]),
             waits=[(sV, CONST_V)] if a == 0 else [], inc=dXZ, inck=16)
    XB_ZERO = dXZ.n

    bps = PS[:, 0:128]
    bias_jobs = []
    bias_jobs.append((dfcqb.rearrange("(r j) -> r j", j=128), 8, fcqb[:, :]))
    bias_jobs.append((dfckb.rearrange("(r j) -> r j", j=128), 8, fckb[:, :]))
    bias_jobs.append((db1.rearrange("e (r j) -> (e r) j", j=128)[0:128, :], 128,
                      b1sb[:, 0:128]))
    bias_jobs.append((db1.rearrange("e (r j) -> (e r) j", j=128)[128:256, :], 128,
                      b1sb[:, 128:256]))
    bstage_ev = 0
    p.wait("tensor", sV, CONST_V)
    for (srcap, nrows, dstap) in bias_jobs:
        p.op("sync", lambda h, srcap=srcap, nrows=nrows: h.dma_start(
            out=xtp[0:nrows, 0:128], in_=srcap),
            waits=[(sV, bstage_ev)], inc=dBS, inck=16)
        ldv = dBS.n
        p.op("tensor", lambda h, nrows=nrows: h.transpose(
            bps[:, 0:nrows], xtp[0:nrows, 0:128], identf[0:nrows, 0:nrows]),
            waits=[(dBS, ldv), (sV, bstage_ev)], inc=sT)
        mmv = sT.n
        p.op("vector", lambda h, dstap=dstap, nrows=nrows: h.tensor_copy(
            dstap, bps[:, 0:nrows]), waits=[(sT, mmv)], inc=sV)
        bstage_ev = sV.n
    BIAS_V = sV.n

    tps = [PS[:, 2048:2176], PS[:, 2560:2688],
           PS[:, 3072:3200], PS[:, 3584:3712]]
    evA = [("V", BIAS_V), ("V", BIAS_V), ("V", BIAS_V), ("V", BIAS_V)]
    p.wait("tensor", sV, CONST_V)
    XT_FC = [0] * FC
    for i, (fc, tc) in enumerate([(a, b_) for a in range(FC) for b_ in range(TC)]):
        sl = i % 4
        ps = tps[sl]
        evw = ([] if evA[sl] == 0 else
               ([(sV, evA[sl][1])] if evA[sl][0] == "V" else [(sS, evA[sl][1])]))
        p.op("tensor", lambda h, ps=ps, tc=tc, fc=fc: h.transpose(
            ps, Xtok[:, tc * D + fc * 128: tc * D + (fc + 1) * 128], identf[:, :]),
            waits=evw + ([(dZ, Z_CHUNK[tc // 2])] if fc == 0 else []),
            inc=sT)
        mmv = sT.n
        p.op("vector", lambda h, ps=ps, tc=tc, fc=fc: h.tensor_copy(
            XT[:, fc * T + tc * 128: fc * T + (tc + 1) * 128], ps),
            waits=[(sT, mmv)], inc=sV)
        evA[sl] = ("V", sV.n)
        if tc == TC - 1:
            XT_FC[fc] = (sV.n, 0)
    XT_V = sV.n

    zpsum = PS[0:64, 0:2048]
    p.wait("tensor", dWL, W_LOADS)
    p.wait("scalar", dWL, W_LOADS)
    zev = 0
    for (wt, zt) in ((wqb, zqT), (wkb, zkT), (wvb, zvT)):
        p.wait("tensor", sV, zev)
        for fc in range(FC):
            p.wait("tensor", sV, XT_FC[fc][0])
            for ns in range(4):
                p.op("tensor", lambda h, wt=wt, fc=fc, ns=ns: h.matmul(
                    zpsum[:, ns * 512:(ns + 1) * 512],
                    wt[:, fc * DK:(fc + 1) * DK],
                    XT[:, fc * T + ns * 512: fc * T + (ns + 1) * 512],
                    start=(fc == 0), stop=(fc == FC - 1)),
                    inc=(sT if (fc == FC - 1 and ns == 3) else None))
        mmv = sT.n
        p.op("vector", lambda h, zt=zt: h.tensor_copy(zt[:, :], zpsum),
             waits=[(sT, mmv)], inc=sV)
        zev = sV.n
    ZPROJ_V = sV.n
    ZPROJ_T = sT.n

    qkpsum = PS[:, 0:2048]
    vps = [PS[:, 0:128], PS[:, 512:640]]
    SPb = [PS[:, 0:1024], PS[:, 1024:2048]]
    PVp = [PS[0:65, 2048:3072], PS[0:65, 3072:4096]]

    qk_ev = [ZPROJ_V, ZPROJ_V]
    pt_free = [0, 0]
    sp_free = [0, 0]

    vo_evs = [0, 0]
    pend_epi = [None]

    def emit_pv(vost, pts, exp_s, jc, par):
        PV = PVp[par]
        p.op("tensor", lambda h, vost=vost, pts=pts, jc=jc, PV=PV: h.matmul(
            PV[:, 0:512], vost, Pt[:, pts * 1024:pts * 1024 + 512],
            start=(jc == 0), stop=(jc == TC - 1)),
            waits=[(sS, exp_s)] + ([(sV, pv_done_par[par])] if jc == 0 else []))
        p.op("tensor", lambda h, vost=vost, pts=pts, jc=jc, PV=PV: h.matmul(
            PV[:, 512:1024], vost,
            Pt[:, pts * 1024 + 512:pts * 1024 + 1024],
            start=(jc == 0), stop=(jc == TC - 1)), inc=sT)
        pt_free[pts] = sT.n

    def mk_epilogue(pair, hh, ipass, pv_t, par):
        st = {}

        def emitA():
            PV = PVp[par]
            icol = ipass * 1024
            p.op("vector", lambda h, PV=PV: h.reciprocal(rb[64:65, 0:1024],
                                                         PV[64:65, 0:1024]),
                 waits=[(sT, pv_t), (dBC, 0)], inc=sV)
            rcb = sV.n
            dvc = pair
            if hh == 0:
                dst = XT[0:64, dvc * T + icol: dvc * T + icol + 1024]
            else:
                dst = aoscr[:, :]
            p.op("vector", lambda h, dst=dst, PV=PV: h.tensor_copy(
                dst, PV[0:64, 0:1024]),
                waits=[(dAO, ao_dmah[0])], inc=sV)
            pv_done_par[par] = sV.n
            st["dst"] = dst
            st["cp"] = sV.n
            p.op("gpsimd", lambda h: h.dma_start(out=drsc[None, :],
                                                 in_=rb[64:65, 0:1024]),
                 waits=[(sV, rcb)], inc=dBC, inck=16)
            p.op("gpsimd", lambda h: h.dma_start(
                out=rb[0:64, 0:1024],
                in_=drsc[None, :].broadcast_to([64, 1024])),
                 waits=[(sV, pv_done_vh[0]), (dBC, 0)], inc=dBC, inck=16)
            st["bc"] = dBC.n

        def emitB():
            dst = st["dst"]
            p.op("vector", lambda h, dst=dst: h.tensor_mul(
                dst, dst, rb[0:64, 0:1024]),
                waits=[(dBC, st["bc"]), (sV, st["cp"])], inc=sV)
            pv_done_vh[0] = sV.n
            if hh == 1:
                dvc = pair
                icol = ipass * 1024
                xdst = XT[64:128, dvc * T + icol: dvc * T + icol + 1024]
                p.op("gpsimd", lambda h, xdst=xdst: h.dma_start(
                    out=xdst, in_=aoscr[:, :]),
                    waits=[(sV, pv_done_vh[0])], inc=dAO, inck=16)
                ao_dmah[0] = dAO.n

        return emitA, emitB

    pv_done_vh = [0]
    pv_done_par = [0, 0]
    pb_gh = [0]
    ao_dmah = [0]
    pend_epiB = [None]
    p.wait("vector", dXZ, XB_ZERO)

    kbufA = [A3[:, 0:T], A3[:, 2 * T:3 * T]]
    kbufB = [A3[:, T:2 * T], A3[:, 3 * T:4 * T]]
    vbuf = [VOh2, A3[:, 16384:18464]]
    kpadA = kbufA[0]
    kpadB = kbufB[0]
    wdead = {}
    gates = {}
    KPREP_V = {}
    VPREP_V = {}

    def mk_kv_items(np):
        kq = deque()
        vq = deque()
        kmm = [0] * 4
        kev = [0] * 4
        vmm = [0] * TC
        vev = [0] * TC
        BA = kbufA[np % 2]
        BB = kbufB[np % 2]
        vb = vbuf[np % 2]
        for r in range(4):
            def kmm_go(r=r):
                w = ([(sV, gates["k"])] if r < 2 else [(sV, kev[r - 2])])
                p.op("tensor", lambda h, r=r, np=np: h.matmul(
                    PS[:, 3072 + (r % 2) * 512:3072 + (r % 2) * 512 + 512],
                    fckwp[:, np * 128:(np + 1) * 128],
                    zkTp[:, r * 512:(r + 1) * 512], start=True, stop=True),
                    waits=w, inc=sT)
                kmm[r] = sT.n
                if np == NH // 2 - 1:
                    wdead["fck"] = sT.n
            kq.append(kmm_go)

            def kev_go(r=r):
                ps = PS[:, 3072 + (r % 2) * 512:3072 + (r % 2) * 512 + 512]
                p.op("vector", lambda h, r=r, np=np, ps=ps: h.tensor_scalar(
                    out=BA[0:64, r * 512:(r + 1) * 512], in0=ps[0:64, :],
                    scalar1=fckb[0:64, np:np + 1], scalar2=None, op0=ALU.add),
                    waits=[(sT, kmm[r])], inc=sV)
                p.op("vector", lambda h, r=r, np=np, ps=ps: h.tensor_scalar(
                    out=BB[64:128, r * 512:(r + 1) * 512], in0=ps[64:128, :],
                    scalar1=fckb[64:128, np:np + 1], scalar2=None, op0=ALU.add),
                    inc=sV)
                kev[r] = sV.n
                KPREP_V[np] = sV.n
            kq.append(kev_go)
        vbias = fcvb128[:, np * 128:(np + 1) * 128].rearrange(
            "p (hh j) -> p hh j", hh=2)
        for tc in range(TC):
            def vmm_go(tc=tc):
                slot = PS[:, 2048 + (tc % 2) * 512:2048 + (tc % 2) * 512 + 128]
                w = ([(sV, gates["v"])] if tc < 2 else [(sV, vev[tc - 2])])
                p.op("tensor", lambda h, tc=tc, np=np, slot=slot: h.matmul(
                    slot, zvTp[:, tc * 128:(tc + 1) * 128],
                    fcvwp[:, np * 128:(np + 1) * 128], start=True, stop=True),
                    waits=w, inc=sT)
                vmm[tc] = sT.n
                if np == NH // 2 - 1:
                    wdead["fcv"] = sT.n
            vq.append(vmm_go)

            def vev_go(tc=tc):
                slot = PS[:, 2048 + (tc % 2) * 512:2048 + (tc % 2) * 512 + 128]
                p.op("vector", lambda h, tc=tc, slot=slot: h.tensor_add(
                    vb[:, :].rearrange("p (hh c j) -> p hh c j",
                                       hh=2, c=TC)[:, :, tc, 0:64],
                    slot.rearrange("p (hh j) -> p hh j", hh=2),
                    vbias),
                    waits=[(sT, vmm[tc])], inc=sV)
                vev[tc] = sV.n
                VPREP_V[np] = sV.n
            vq.append(vev_go)
        return kq, vq

    for pair in range(NH // 2):
        qsl = qT2[:, :]
        p.wait("tensor", sS, sp_free[0])
        p.wait("tensor", sS, sp_free[1])
        p.wait("tensor", sV, qk_ev[0])
        if pair in VPREP_V:
            p.wait("tensor", sV, VPREP_V[pair])
        for ns in range(4):
            p.op("tensor", lambda h, ns=ns, pair=pair: h.matmul(
                qkpsum[:, ns * 512:(ns + 1) * 512],
                fcqwp[:, pair * 128:(pair + 1) * 128],
                zqTp[:, ns * 512:(ns + 1) * 512], start=True, stop=True),
                inc=(sT if ns == 3 else None))
        mmv = sT.n
        if pair == NH // 2 - 1:
            wdead["fcq"] = mmv
        p.op("vector", lambda h, pair=pair: h.tensor_scalar(
            out=qsl, in0=qkpsum, scalar1=fcqb[:, pair:pair + 1], scalar2=None,
            op0=ALU.add), waits=[(sT, mmv)], inc=sV)
        p.wait("tensor", sV, sV.n)
        if True:
            for ns in range(4):
                p.op("tensor", lambda h, ns=ns, pair=pair: h.matmul(
                    qkpsum[:, ns * 512:(ns + 1) * 512],
                    fckwp[:, pair * 128:(pair + 1) * 128],
                    zkTp[:, ns * 512:(ns + 1) * 512], start=True, stop=True),
                    inc=(sT if ns == 3 else None))
            mmv = sT.n
            if pair == NH // 2 - 1:
                wdead["fck"] = mmv
            p.op("vector", lambda h, pair=pair: h.tensor_scalar(
                out=kpadA[0:64, :], in0=qkpsum[0:64, :],
                scalar1=fckb[0:64, pair:pair + 1], scalar2=None,
                op0=ALU.add), waits=[(sT, mmv)], inc=sV)
            p.op("vector", lambda h, pair=pair: h.tensor_scalar(
                out=kpadB[64:128, :], in0=qkpsum[64:128, :],
                scalar1=fckb[64:128, pair:pair + 1], scalar2=None,
                op0=ALU.add), inc=sV)
            p.wait("tensor", sV, sV.n)
            vbias0 = fcvb128[:, pair * 128:(pair + 1) * 128].rearrange(
                "p (hh j) -> p hh j", hh=2)
            for tc in range(TC):
                sl = tc % 2
                ps = vps[sl]
                p.op("tensor", lambda h, ps=ps, tc=tc, pair=pair: h.matmul(
                    ps, zvTp[:, tc * 128:(tc + 1) * 128],
                    fcvwp[:, pair * 128:(pair + 1) * 128], start=True,
                    stop=True),
                    waits=[(sV, vo_evs[sl])], inc=sT)
                mmv = sT.n
                p.op("vector", lambda h, ps=ps, tc=tc, vbias0=vbias0:
                     h.tensor_add(
                         VOh2[:, :].rearrange("p (hh c j) -> p hh c j",
                                              hh=2, c=TC)[:, :, tc, 0:64],
                         ps.rearrange("p (hh j) -> p hh j", hh=2),
                         vbias0),
                    waits=[(sT, mmv)], inc=sV)
                vo_evs[sl] = sV.n
            if pair == NH // 2 - 1:
                wdead["fcv"] = sT.n
            p.wait("tensor", sV, sV.n)

        kq, vq = deque(), deque()
        kA = kbufA[0]
        kB = kbufB[0]
        vB = vbuf[0]
        for hh in range(2):
            for ipass in range(2):
                par = (pair * 4 + hh * 2 + ipass) % 2
                icol = ipass * 1024
                host = (kq if (hh == 1 and ipass == 0) else
                        (vq if (hh == 1 and ipass == 1) else None))
                if hh == 1 and ipass == 1 and (pair + 1) in KPREP_V:
                    p.wait("tensor", sV, KPREP_V[pair + 1])
                pend_pv = None
                for jc in range(TC):
                    sp = SPb[jc % 2]
                    pts = jc % 2
                    kst = (kA if hh == 0 else kB)[:, jc * 128:(jc + 1) * 128]
                    qmv = qsl[:, icol:icol + 1024]
                    p.op("tensor", lambda h, sp=sp, kst=kst, qmv=qmv: h.matmul(
                        sp[:, 0:512], kst, qmv[:, 0:512], start=True, stop=True),
                        waits=[(sS, sp_free[jc % 2])])
                    p.op("tensor", lambda h, sp=sp, kst=kst, qmv=qmv: h.matmul(
                        sp[:, 512:1024], kst, qmv[:, 512:1024],
                        start=True, stop=True), inc=sT)
                    qk_t = sT.n
                    if (pair == NH // 2 - 1 and hh == 1 and ipass == 1
                            and jc == TC - 1):
                        wdead["qT2"] = qk_t
                    if jc == 1 and pend_epi[0] is not None:
                        pend_epi[0][0]()
                        pend_epiB[0] = pend_epi[0][1]
                        pend_epi[0] = None
                        gates["k"] = pv_done_par[1]
                        gates["v"] = pv_done_par[0]
                    if jc == 5 and pend_epiB[0] is not None:
                        pend_epiB[0]()
                        pend_epiB[0] = None
                    p.op("scalar", lambda h, sp=sp, pts=pts: h.activation(
                        Pt[:, pts * 1024:(pts + 1) * 1024], sp, AF.Exp, scale=0.125),
                        waits=[(sT, qk_t), (sT, pt_free[pts])], inc=sS)
                    sp_free[jc % 2] = sS.n
                    exp_s = sS.n
                    if pend_pv is not None:
                        emit_pv(*pend_pv)
                    vost = vB[:, (hh * TC + jc) * 65:(hh * TC + jc + 1) * 65]
                    pend_pv = (vost, pts, exp_s, jc, par)
                    if host is not None and jc >= 4:
                        for _ in range(3):
                            if host:
                                host.popleft()()
                emit_pv(*pend_pv)
                while host:
                    host.popleft()()
                pv_t = sT.n
                pend_epi[0] = mk_epilogue(pair, hh, ipass, pv_t, par)
        qk_ev[0] = pv_done_vh[0]
    if pend_epi[0] is not None:
        pend_epi[0][0]()
        pend_epi[0][1]()
        pend_epi[0] = None
    ATT_V = pv_done_vh[0]
    p.wait("tensor", dAO, ao_dmah[0])
    p.wait("tensor", sV, ATT_V)
    ln1_sq_s = []

    fps = [PS[:, 0:1024], PS[:, 1024:2048]]
    fco_ev = [0, 0]
    ln1_s = 0
    for tc in range(TC):
        ps = fps[tc % 2]
        p.wait("tensor", sV, fco_ev[tc % 2])
        for kc in range(FC):
            ast = XT[:, kc * T + tc * 128: kc * T + (tc + 1) * 128]
            for ns in range(2):
                p.op("tensor", lambda h, ps=ps, ast=ast, kc=kc, ns=ns: h.matmul(
                    ps[:, ns * 512:(ns + 1) * 512], ast,
                    A3[:, 8192 + kc * 1024 + ns * 512:
                       8192 + kc * 1024 + (ns + 1) * 512],
                    start=(kc == 0), stop=False))
        for ns in range(2):
            p.op("tensor", lambda h, ps=ps, ns=ns: h.matmul(
                ps[:, ns * 512:(ns + 1) * 512], onesrow[:, :],
                fcobr[:, ns * 512:(ns + 1) * 512], start=False, stop=True),
                inc=(sT if ns == 1 else None))
        mmv = sT.n
        p.op("vector", lambda h, ps=ps, tc=tc: h.scalar_tensor_tensor(
            out=Xtok[:, tc * D:(tc + 1) * D], in0=ps, scalar=1.0,
            in1=Xtok[:, tc * D:(tc + 1) * D], op0=ALU.mult, op1=ALU.add,
            accum_out=msum[:, tc:tc + 1]),
            waits=[(sT, mmv)], inc=sV)
        fco_ev[tc % 2] = sV.n
        prev = sV.n
        p.op("scalar", lambda h, tc=tc: h.activation(
            sqscr[:, :], Xtok[:, tc * D:(tc + 1) * D], AF.Square,
            accum_out=sumsq[:, tc:tc + 1]),
            waits=[(sV, prev), (sS, ln1_s)], inc=sS)
        ln1_s = sS.n
        ln1_sq_s.append(sS.n)
    FCO_V = sV.n
    FCO_T = sT.n

    def ln_stats(lo, hi, wait_s, wait_v):
        g = slice(lo, hi)
        p.op("vector", lambda h, g=g: h.tensor_scalar(
            out=mmean[:, g], in0=msum[:, g], scalar1=1.0 / D, scalar2=None,
            op0=ALU.mult), waits=[(sS, wait_s), (sV, wait_v)], inc=sV)
        p.op("vector", lambda h, g=g: h.tensor_mul(meansq[:, g], mmean[:, g],
                                                   mmean[:, g]),
             waits=[(sV, sV.n)], inc=sV)
        p.op("vector", lambda h, g=g: h.tensor_scalar(
            out=varv[:, g], in0=sumsq[:, g], scalar1=1.0 / D, scalar2=None,
            op0=ALU.mult), inc=sV)
        p.op("vector", lambda h, g=g: h.tensor_sub(varv[:, g], varv[:, g],
                                                   meansq[:, g]),
             waits=[(sV, sV.n)], inc=sV)
        vv = sV.n
        p.op("scalar", lambda h, g=g: h.activation(sqv[:, g], varv[:, g],
                                                   AF.Sqrt,
                                                   bias=epsc[:, :], scale=1.0),
             waits=[(sV, vv)], inc=sS)
        p.op("vector", lambda h, g=g: h.reciprocal(rstd[:, g], sqv[:, g]),
             waits=[(sS, sS.n)], inc=sV)
        return sS.n, sV.n

    x2_ev = []
    for grp in range(4):
        _, gv = ln_stats(grp * 4, (grp + 1) * 4, ln1_sq_s[grp * 4 + 3], FCO_V)
        for tc in range(grp * 4, (grp + 1) * 4):
            p.op("vector", lambda h, tc=tc: h.tensor_scalar(
                out=Xtok[:, tc * D:(tc + 1) * D],
                in0=Xtok[:, tc * D:(tc + 1) * D],
                scalar1=mmean[:, tc:tc + 1], scalar2=rstd[:, tc:tc + 1],
                op0=ALU.subtract, op1=ALU.mult),
                waits=[(sV, gv), (sS, ln1_sq_s[tc])], inc=sV)
            x2_ev.append(sV.n)
    X2_V = sV.n

    rps = [PS[:, 0:128], PS[:, 512:640]]
    rrp = PS[:, 1024:1032]
    ps_ev = [0, 0]
    xtp_used = [0, 0]
    rr_ev = 0
    for tc in range(TC):
        for fc in range(FC):
            sl = fc % 2
            ps = rps[sl]
            p.op("tensor", lambda h, ps=ps, tc=tc, fc=fc: h.transpose(
                ps, Xtok[:, tc * D + fc * 128: tc * D + (fc + 1) * 128],
                identf[:, :]),
                waits=[(sV, ps_ev[sl])] + ([(sV, x2_ev[tc])] if fc == 0 else []),
                inc=sT)
            mmv = sT.n
            p.op("vector", lambda h, ps=ps, sl=sl: h.tensor_copy(
                xtp[:, sl * 128:(sl + 1) * 128], ps),
                waits=[(sT, mmv), (sT, xtp_used[sl])], inc=sV)
            ps_ev[sl] = sV.n
            tpv = sV.n
            p.op("tensor", lambda h, fc=fc, sl=sl: h.matmul(
                rrp, xtp[:, sl * 128:(sl + 1) * 128],
                wrb[:, fc * E:(fc + 1) * E], start=(fc == 0), stop=(fc == FC - 1)),
                waits=[(sV, tpv)] + ([(sV, rr_ev)] if fc == 0 else []),
                inc=sT)
            xtp_used[sl] = sT.n
        mmv = sT.n
        p.op("vector", lambda h, tc=tc: h.tensor_copy(rr[:, tc * E:(tc + 1) * E],
                                                      rrp),
             waits=[(sT, mmv)], inc=sV)
        rr_ev = sV.n
    RR_V = sV.n

    W1SLOT = [WA2[:, 0:1024], WA2[:, 1024:2048], WA2[:, 2048:3072],
              qT2[:, 0:1024]]
    W2SLOT = [WA2[:, 5120:6144], WA2[:, 4096:5120], WA2[:, 3072:4096],
              qT2[:, 1024:2048]]
    w1_ld = [ZPROJ_T, wdead["fcq"], wdead["fck"], wdead["qT2"]]
    w2_ld = [0, wdead["fcv"], wdead["fcv"], wdead["qT2"]]
    w1_ldv = {}
    w2_ldv = {}

    def w1_load(e, hc):
        wsl = hc % 4
        p.op("gpsimd", lambda h, e=e, hc=hc, wsl=wsl: h.dma_start(
            out=W1SLOT[wsl], in_=dw1[e, hc, :, :]),
            waits=[(sT, w1_ld[wsl])], inc=dW1[wsl], inck=16)
        w1_ldv[(e, hc)] = dW1[wsl].n

    def w2_load(e, hc):
        wsl = hc % 4
        p.op("gpsimd", lambda h, e=e, hc=hc, wsl=wsl: h.dma_start(
            out=W2SLOT[wsl], in_=dw2[e, hc * 128:(hc + 1) * 128, :]),
            waits=[(sT, w2_ld[wsl])], inc=dW2[wsl], inck=16)
        w2_ldv[(e, hc)] = dW2[wsl].n

    for hc0 in range(4):
        w1_load(0, hc0)
        w2_load(0, hc0)

    p.wait("vector", sG, CONST_G)
    scr1 = A("scr1", [128, 128], F32)
    scr2 = A("scr2", [128, 128], F32)
    cum = A("cum", [128, 128], F32)
    m1s = A("m1s", [128, 16], F32)
    m2s = A("m2s", [128, 16], F32)
    e1c = A("e1c", [128, 16], F32)
    e2c = A("e2c", [128, 16], F32)
    sumE = A("sumE", [128, 16], F32)
    rcpE = A("rcpE", [128, 16], F32)
    p1c = A("p1c", [128, 16], F32)
    p2c = A("p2c", [128, 16], F32)
    pos1 = A("pos1", [128, 16], F32)
    pos2 = A("pos2", [128, 16], F32)
    keep1 = A("keep1", [128, 16], F32)
    keep2 = A("keep2", [128, 16], F32)
    posc1 = A("posc1", [128, 16], F32)
    posc2 = A("posc2", [128, 16], F32)
    slt = A("slt", [128, 16], F32)
    iota128 = A("iota128", [128, 128], F32)
    p.op("gpsimd", lambda h: h.iota(iota128[:, :], pattern=[[0, 16], [1, 8]],
                                    base=0, channel_multiplier=0,
                                    allow_small_or_imprecise_dtypes=True), inc=sG)
    IOTA_G = sG.n
    rr3 = rr[:, :].rearrange("p (t e) -> p t e", e=E)

    def seg(x):
        return x[:, :].rearrange("p (t e) -> p t e", e=E)

    def segb(x16):
        return x16[:, :, None].broadcast_to([128, 16, E])

    def vop(build, waits=()):
        p.op("vector", build, waits=waits, inc=sV)
        p.wait("vector", sV, sV.n)

    def sel_block(msrc, rsrc, eqbuf, selout):
        vop(lambda h: h.reduce_max(msrc[:, :], seg(rsrc), axis=AX.X))
        vop(lambda h: h.tensor_tensor(seg(eqbuf), seg(rsrc), segb(msrc),
                                      ALU.is_equal))
        vop(lambda h: h.tensor_copy(cum[:, :], eqbuf[:, :]))
        for sh in (1, 2, 4):
            vop(lambda h, sh=sh: h.tensor_copy(
                seg(scr2)[:, :, 0:sh], seg(cum)[:, :, 0:sh]))
            vop(lambda h, sh=sh: h.tensor_add(
                seg(scr2)[:, :, sh:E], seg(cum)[:, :, sh:E],
                seg(cum)[:, :, 0:E - sh]))
            vop(lambda h: h.tensor_copy(cum[:, :], scr2[:, :]))
        vop(lambda h: h.tensor_scalar(out=scr2[:, :], in0=cum[:, :], scalar1=1.0,
                                      scalar2=None, op0=ALU.is_equal))
        vop(lambda h: h.tensor_mul(selout[:, :], eqbuf[:, :], scr2[:, :]))

    p.wait("vector", sV, RR_V)
    p.wait("vector", sG, IOTA_G)
    sel_block(m1s, rr, scr1, sel1)
    vop(lambda h: h.tensor_scalar(out=scr1[:, :], in0=sel1[:, :], scalar1=1e30,
                                  scalar2=None, op0=ALU.mult))
    vop(lambda h: h.tensor_sub(scr2[:, :], rr[:, :], scr1[:, :]))
    sel_block(m2s, scr2, scr1, sel2)
    vop(lambda h: h.tensor_mul(scr1[:, :], sel1[:, :], iota128[:, :]))
    vop(lambda h: h.reduce_sum(e1c[:, :], seg(scr1), axis=AX.X))
    vop(lambda h: h.tensor_mul(scr1[:, :], sel2[:, :], iota128[:, :]))
    vop(lambda h: h.reduce_sum(e2c[:, :], seg(scr1), axis=AX.X))
    vop(lambda h: h.tensor_tensor(seg(scr1), rr3, segb(m1s), ALU.subtract))
    p.op("scalar", lambda h: h.activation(e8[:, :], scr1[:, :], AF.Exp,
                                          scale=1.0), waits=[(sV, sV.n)], inc=sS)
    p.wait("vector", sS, sS.n)
    vop(lambda h: h.reduce_sum(sumE[:, :], seg(e8), axis=AX.X))
    vop(lambda h: h.reciprocal(rcpE[:, :], sumE[:, :]))
    vop(lambda h: h.tensor_mul(scr1[:, :], e8[:, :], sel1[:, :]))
    vop(lambda h: h.reduce_sum(p1c[:, :], seg(scr1), axis=AX.X))
    vop(lambda h: h.tensor_mul(p1c[:, :], p1c[:, :], rcpE[:, :]))
    vop(lambda h: h.tensor_mul(scr1[:, :], e8[:, :], sel2[:, :]))
    vop(lambda h: h.reduce_sum(p2c[:, :], seg(scr1), axis=AX.X))
    vop(lambda h: h.tensor_mul(p2c[:, :], p2c[:, :], rcpE[:, :]))
    vop(lambda h: h.tensor_add(selb[:, :], sel1[:, :], sel2[:, :]))
    SELB_V = sV.n

    ccp = PS[0:1, 1536:1664]
    p.op("tensor", lambda h: h.matmul(ccp, onesb[:, 0:1], selb[:, :],
                                      start=True, stop=True),
         waits=[(sV, SELB_V)], inc=sT)
    mmv = sT.n
    p.op("vector", lambda h: h.tensor_copy(csumf[:, :], ccp),
         waits=[(sT, mmv)], inc=sV)
    p.wait("vector", sV, sV.n)
    p.op("vector", lambda h: h.memset(pfA[:, 0:8], 0.0), inc=sV)
    p.op("vector", lambda h: h.tensor_copy(pfA[:, 8:128], csumf[:, 0:120]),
         inc=sV)
    p.wait("vector", sV, sV.n)
    srcb_, dstb_ = pfA, pfB
    for sh in (8, 16, 32, 64):
        p.op("vector", lambda h, s=srcb_, d=dstb_, sh=sh: h.tensor_copy(
            d[:, 0:sh], s[:, 0:sh]), inc=sV)
        p.op("vector", lambda h, s=srcb_, d=dstb_, sh=sh: h.tensor_add(
            d[:, sh:128], s[:, sh:128], s[:, 0:128 - sh]), inc=sV)
        p.wait("vector", sV, sV.n)
        srcb_, dstb_ = dstb_, srcb_
    p.op("vector", lambda h, s=srcb_: h.tensor_copy(cpflat[:, :], s[:, :]),
         inc=sV)
    CP_V = sV.n

    npp = PS[:, 1024:1152]
    p.op("tensor", lambda h: h.matmul(npp, u128[:, :], selb[:, :],
                                      start=True, stop=False),
         waits=[(sV, CP_V)])
    p.op("tensor", lambda h: h.matmul(npp, onesrow[:, :], cpflat[:, :],
                                      start=False, stop=True), inc=sT)
    mmv = sT.n
    p.op("vector", lambda h: h.tensor_copy(scr2[:, :], npp),
         waits=[(sT, mmv)], inc=sV)
    p.wait("vector", sV, sV.n)
    for k, (sel, e_c, p_c, posk, keepk, posck) in enumerate((
            (sel1, e1c, p1c, pos1, keep1, posc1),
            (sel2, e2c, p2c, pos2, keep2, posc2))):
        vop(lambda h, sel=sel: h.tensor_mul(scr1[:, :], sel[:, :], scr2[:, :]))
        vop(lambda h, posk=posk: h.reduce_sum(posk[:, :], seg(scr1), axis=AX.X))
        vop(lambda h, posk=posk, keepk=keepk: h.tensor_scalar(
            out=keepk[:, :], in0=posk[:, :], scalar1=float(CAP), scalar2=None,
            op0=ALU.is_lt))
        vop(lambda h, posk=posk, posck=posck: h.tensor_scalar_min(
            posck[:, :], posk[:, :], float(CAP - 1)))
        vop(lambda h, e_c=e_c, posck=posck: h.scalar_tensor_tensor(
            out=slt[:, :], in0=e_c[:, :], scalar=float(CAP), in1=posck[:, :],
            op0=ALU.mult, op1=ALU.add))
        vop(lambda h, k=k: h.tensor_copy(
            gidx[:, :].rearrange("p (t two) -> p t two", two=2)[:, :, k:k + 1],
            slt[:, :, None]))
        vop(lambda h, p_c=p_c, keepk=keepk, k=k: h.tensor_mul(
            coefb[:, :].rearrange("p (t two) -> p t two", two=2)[:, :, k],
            p_c[:, :], keepk[:, :]))
        vop(lambda h: h.tensor_scalar(out=scr1[:, 0:16], in0=slt[:, :],
                                      scalar1=float(DUMP), scalar2=None,
                                      op0=ALU.subtract))
        vop(lambda h, keepk=keepk: h.tensor_mul(scr1[:, 0:16], scr1[:, 0:16],
                                                keepk[:, :]))
        vop(lambda h: h.tensor_scalar(out=scr1[:, 0:16], in0=scr1[:, 0:16],
                                      scalar1=float(DUMP), scalar2=None,
                                      op0=ALU.add))
        vop(lambda h, k=k: h.tensor_copy(
            sidx[:, :].rearrange("p (t two) -> p t two", two=2)[:, :, k:k + 1],
            scr1[:, 0:16, None]))
    ROUTE_V = sV.n
    for tc in range(TC):
        for k in range(2):
            p.op("gpsimd", lambda h, tc=tc, k=k: h.indirect_dma_start(
                out=dxb[:, :],
                out_offset=IndirectOffsetOnAxis(
                    ap=sidx[:, 2 * tc + k:2 * tc + k + 1], axis=0),
                in_=Xtok[:, tc * D:(tc + 1) * D], in_offset=None,
                bounds_check=DUMP - 1, oob_is_err=False),
                waits=[(sV, ROUTE_V), (dSCT, dSCT.n), (dXZ, XB_ZERO)],
                inc=dSCT, inck=16)
    SCAT_DMA = dSCT.n


    dXB = p.sem("dXB", dma=True)
    SC = DUMP // 128
    p.wait("sync", dSCT, SCAT_DMA)
    p.wait("sync", sT, FCO_T)
    p.wait("sync", sV, FCO_V)
    for fc in range(FC):
        p.op("sync", lambda h, fc=fc: h.dma_start(
            out=A3[:, fc * DUMP:(fc + 1) * DUMP],
            in_=dxb[0:DUMP, fc * 128:(fc + 1) * 128], transpose=True),
            inc=dXB, inck=16)
    XBUF_D = dXB.n

    hps = [PS[:, 0:512], PS[:, 512:1024]]

    yps = [PS[:, 1024:2048], PS[:, 2048:3072], PS[0:64, 3072:4096]]
    h_ev = [0, 0]
    y_ev = [0, 0, 0]
    yst = [0, 0]
    hT_busy_t = 0
    b2_used = [0, 0]
    relu_s = {}

    for e in range(E):
        p.op("gpsimd", lambda h, e=e: h.dma_start(
            out=b2r[:, :], in_=db2[e, :][None, :]),
            waits=[(sT, b2_used[0])], inc=dB2, inck=16)
        b2_ld = dB2.n
        for step in range(HC + 1):
            if step < HC:
                hc = step
                wsl = hc % 4
                if (e, hc) not in w1_ldv:
                    w1_load(e, hc)
                ldv = w1_ldv[(e, hc)]
                psl = hc % 2
                ps = hps[psl]
                for kc in range(FC):
                    p.op("tensor", lambda h, ps=ps, wsl=wsl, kc=kc, e=e: h.matmul(
                        ps[:, 0:CAP],
                        W1SLOT[wsl][:, kc * 128:(kc + 1) * 128],
                        A3[:, kc * DUMP + e * CAP: kc * DUMP + (e + 1) * CAP],
                        start=(kc == 0), stop=(kc == FC - 1)),
                        waits=([(dW1[wsl], ldv), (sS, h_ev[psl])] if kc == 0 else [])
                              + ([(sT, hT_busy_t), (dXB, XBUF_D)]
                                 if kc == 0 and hc == 0 else []),
                        inc=(sT if kc == FC - 1 else None))
                w1_ld[wsl] = sT.n
                mmv = sT.n
                p.op("scalar", lambda h, ps=ps, e=e, hc=hc: h.activation(
                    XT[:, hc * CAP:(hc + 1) * CAP], ps[:, 0:CAP], AF.Relu,
                    bias=b1sb[:, e * HC + hc:e * HC + hc + 1], scale=1.0),
                    waits=[(sT, mmv)], inc=sS)
                h_ev[psl] = sS.n
                relu_s[(e, hc)] = sS.n
            if step >= 1:
                hc = step - 1
                wsl = hc % 4
                if (e, hc) not in w2_ldv:
                    w2_load(e, hc)
                ldv = w2_ldv[(e, hc)]
                for si, (s0, s1) in enumerate(((0, 128), (128, 256), (256, 320))):
                    for ns in range(2):
                        p.op("tensor", lambda h, hc=hc, wsl=wsl, si=si, s0=s0,
                             s1=s1, ns=ns: h.matmul(
                            yps[si][0:(s1 - s0), ns * 512:(ns + 1) * 512],
                            XT[0:128, hc * CAP + s0: hc * CAP + s1],
                            W2SLOT[wsl][:, ns * 512:(ns + 1) * 512],
                            start=(hc == 0), stop=False),
                            waits=([(dW2[wsl], ldv), (sS, relu_s[(e, hc)])]
                                   if ns == 0 and si == 0 else [])
                                  + ([(sV, y_ev[si])] if hc == 0 and ns == 0 else []),
                            inc=(sT if (si == 2 and ns == 1) else None))
                w2_ld[wsl] = sT.n
        for si, (s0, s1) in enumerate(((0, 128), (128, 256), (256, 320))):
            for ns in range(2):
                p.op("tensor", lambda h, si=si, s0=s0, s1=s1, ns=ns: h.matmul(
                    yps[si][0:(s1 - s0), ns * 512:(ns + 1) * 512],
                    onesrow[0:1, 0:(s1 - s0)],
                    b2r[:, ns * 512:(ns + 1) * 512],
                    start=False, stop=True),
                    waits=[(dB2, b2_ld)] if si == 0 and ns == 0 else [],
                    inc=(sT if ns == 1 else None))
            b2_used[0] = sT.n
            mmv = sT.n
            ysl = si % 2
            p.op("vector", lambda h, si=si, s0=s0, s1=s1, ysl=ysl: h.tensor_copy(
                ysb[0:(s1 - s0), ysl * 1024:(ysl + 1) * 1024], yps[si][0:(s1 - s0), :]),
                waits=[(sT, mmv), (dY[ysl], yst[ysl])], inc=sV)
            y_ev[si] = sV.n
            p.op("scalar", lambda h, e=e, s0=s0, s1=s1, ysl=ysl: h.dma_start(
                out=dyb[e * CAP + s0: e * CAP + s1, :],
                in_=ysb[0:(s1 - s0), ysl * 1024:(ysl + 1) * 1024]),
                waits=[(sV, y_ev[si])], inc=dY[ysl], inck=16)
            yst[ysl] = dY[ysl].n
        hT_busy_t = sT.n
    Y_DMA0, Y_DMA1 = dY[0].n, dY[1].n

    p.wait("gpsimd", dY[0], Y_DMA0)
    p.wait("gpsimd", dY[1], Y_DMA1)
    ln2_s = 0
    ln2_sq_s = []
    comb_v = [0, 0]
    g_dma = []
    for tc in range(TC):
        sl = tc % 2
        for k, gt in enumerate((G0, G1)):
            p.op("gpsimd", lambda h, tc=tc, k=k, sl=sl, gt=gt: h.indirect_dma_start(
                out=gt[:, sl * 1024:(sl + 1) * 1024], out_offset=None,
                in_=dyb[0:DUMP, :],
                in_offset=IndirectOffsetOnAxis(
                    ap=gidx[:, 2 * tc + k:2 * tc + k + 1], axis=0)),
                waits=[(sV, comb_v[sl])], inc=dGA[k * 2 + sl], inck=16)
        g_dma.append((dGA[0 + sl].n, dGA[2 + sl].n))
        p.op("vector", lambda h, tc=tc, sl=sl: h.scalar_tensor_tensor(
            out=Xtok[:, tc * D:(tc + 1) * D], in0=G1[:, sl * 1024:(sl + 1) * 1024],
            scalar=coefb[:, 2 * tc + 1:2 * tc + 2],
            in1=Xtok[:, tc * D:(tc + 1) * D], op0=ALU.mult, op1=ALU.add),
            waits=[(dGA[0 + sl], g_dma[tc][0]), (dGA[2 + sl], g_dma[tc][1])],
            inc=sV)
        p.wait("vector", sV, sV.n)
        p.op("vector", lambda h, tc=tc, sl=sl: h.scalar_tensor_tensor(
            out=Xtok[:, tc * D:(tc + 1) * D], in0=G0[:, sl * 1024:(sl + 1) * 1024],
            scalar=coefb[:, 2 * tc:2 * tc + 1],
            in1=Xtok[:, tc * D:(tc + 1) * D], op0=ALU.mult, op1=ALU.add,
            accum_out=msum[:, tc:tc + 1]), inc=sV)
        comb_v[sl] = sV.n
        prev = sV.n
        p.op("scalar", lambda h, tc=tc: h.activation(
            sqscr[:, :], Xtok[:, tc * D:(tc + 1) * D], AF.Square,
            accum_out=sumsq[:, tc:tc + 1]),
            waits=[(sV, prev), (sS, ln2_s)], inc=sS)
        ln2_s = sS.n
        ln2_sq_s.append(sS.n)
    COMB_V = sV.n
    for grp in range(4):
        _, gv = ln_stats(grp * 4, (grp + 1) * 4, ln2_sq_s[grp * 4 + 3], 0)
        for tc in range(grp * 4, (grp + 1) * 4):
            p.op("vector", lambda h, tc=tc: h.tensor_scalar(
                out=Xtok[:, tc * D:(tc + 1) * D],
                in0=Xtok[:, tc * D:(tc + 1) * D],
                scalar1=mmean[:, tc:tc + 1], scalar2=rstd[:, tc:tc + 1],
                op0=ALU.subtract, op1=ALU.mult),
                waits=[(sV, gv), (sS, ln2_sq_s[tc])], inc=sV)
            xv = sV.n
            p.op("sync", lambda h, tc=tc: h.dma_start(
                out=dout[tc * 128:(tc + 1) * 128, :],
                in_=Xtok[:, tc * D:(tc + 1) * D]),
                waits=[(sV, xv)], inc=dOUT, inck=16)
    p.wait("sync", dOUT, dOUT.n)

    p.finalize()
    return nc


_NC_CACHE = None


def kernel(**inputs):
    global _NC_CACHE
    if _NC_CACHE is None:
        _NC_CACHE = build_core_program()
    nc = _NC_CACHE

    z = np.ascontiguousarray(np.asarray(inputs["L0_attn_z"], np.float32))
    names = {
        "wq": "L0_attn_wq", "wk": "L0_attn_wk", "wv": "L0_attn_wv",
        "fcq_w": "L0_attn_fcq_w", "fcq_b": "L0_attn_fcq_b",
        "fck_w": "L0_attn_fck_w", "fck_b": "L0_attn_fck_b",
        "fcv_w": "L0_attn_fcv_w", "fcv_b": "L0_attn_fcv_b",
        "fco_w": "L0_attn_fco_w", "fco_b": "L0_attn_fco_b",
        "Wr": "L0_ffn_Wr", "W1": "L0_ffn_W1", "b1": "L0_ffn_b1",
        "W2": "L0_ffn_W2", "b2": "L0_ffn_b2",
    }
    shared = {k: np.ascontiguousarray(np.asarray(inputs[v], np.float32))
              for k, v in names.items()}
    shared["W1"] = np.ascontiguousarray(
        shared["W1"].reshape(E, FC, 128, HC, 128).transpose(0, 3, 2, 1, 4)
        .reshape(E, HC, 128, FC * 128))
    in_maps = []
    for b in range(B):
        m = dict(shared)
        m["z"] = np.ascontiguousarray(z[b])
        in_maps.append(m)
    res = run_bass_kernel_spmd(nc, in_maps, core_ids=list(range(B)))
    out = np.stack([res.results[b]["o"] for b in range(B)], axis=0)
    return out.astype(np.float32)
